# version 06
# baseline (speedup 1.0000x reference)
import contextlib
import numpy as np
import concourse.bass as bass
import concourse.mybir as mybir
from concourse.bass_utils import run_bass_kernel_spmd

F32 = mybir.dt.float32
BF16 = mybir.dt.bfloat16
I32 = mybir.dt.int32
AF = mybir.ActivationFunctionType
ALU = mybir.AluOpType

D = 1024
S = 2048
NSEQ = 4
FF = 3072
EPS = 1e-5
QSC = 1.0 / 127.0
SPREAD_OPROJ = True
PVLAG = True
E4BF16 = True


class PSem:
    __slots__ = ("name", "count", "handle")

    def __init__(self, name):
        self.name = name
        self.count = 0
        self.handle = None


class Plan:
    def __init__(self):
        self.steps = []
        self.sems = []

    def sem(self, name):
        s = PSem(name)
        self.sems.append(s)
        return s

    def step(self, engine, fn, wait=(), inc=(), tag=""):
        waits = []
        for w in wait:
            if w is None:
                continue
            if isinstance(w, PSem):
                waits.append((w, w.count))
            else:
                waits.append(w)
        incs = list(inc)
        for sm, amt in incs:
            sm.count += amt
        self.steps.append((engine, fn, waits, incs, tag))

    def emit(self, nc, es):
        for sm in self.sems:
            sm.handle = es.enter_context(nc.semaphore(sm.name))
        block = es.enter_context(nc.Block())
        steps = self.steps

        self.emitted = [None] * len(steps)

        def mk(engine_name):
            def body(eng):
                waited = {}
                for si, (engine, fn, waits, incs, tag) in enumerate(steps):
                    if engine != engine_name:
                        continue
                    for sm, val in waits:
                        if val > 0 and waited.get(id(sm), -1) < val:
                            eng.wait_ge(sm.handle, val)
                            waited[id(sm)] = val
                    inst = fn(eng)
                    self.emitted[si] = inst
                    if incs:
                        assert inst is not None, f"step {tag} needs inst for incs"
                        for sm, amt in incs:
                            inst = inst.then_inc(sm.handle, amt)
            return body

        used = {st[0] for st in steps}
        for name, opener in (("sync", block.sync), ("tensor", block.tensor),
                             ("vector", block.vector), ("scalar", block.scalar),
                             ("gpsimd", block.gpsimd)):
            if name in used:
                opener(mk(name))


def build_program(truncate=None, truncate_at=None, debug_scratch=False):
    nc = bass.Bass("TRN2", target_bir_lowering=False)

    X = nc.dram_tensor("x", [NSEQ * S, D], F32, kind="ExternalInput").ap()
    OUT = nc.dram_tensor("out", [NSEQ * S, D], F32, kind="ExternalOutput").ap()
    WQ = nc.dram_tensor("wq", [D, D], F32, kind="ExternalInput").ap()
    WK = nc.dram_tensor("wk", [D, D], F32, kind="ExternalInput").ap()
    WV = nc.dram_tensor("wv", [D, D], F32, kind="ExternalInput").ap()
    WO = nc.dram_tensor("wo", [D, D], F32, kind="ExternalInput").ap()
    BQ = nc.dram_tensor("bq", [D], F32, kind="ExternalInput").ap()
    BK = nc.dram_tensor("bk", [D], F32, kind="ExternalInput").ap()
    BV = nc.dram_tensor("bv", [D], F32, kind="ExternalInput").ap()
    BO = nc.dram_tensor("bo", [D], F32, kind="ExternalInput").ap()
    WG = nc.dram_tensor("wgi", [FF, D], I32, kind="ExternalInput").ap()
    WU = nc.dram_tensor("wui", [FF, D], I32, kind="ExternalInput").ap()
    WD = nc.dram_tensor("wdi", [D, FF], I32, kind="ExternalInput").ap()

    wbf = {}
    for nm in ("wq", "wk", "wv", "wo"):
        wbf[nm] = nc.dram_tensor(nm + "_bf", [D, D], BF16, kind="Internal").ap()
    wg_bf = nc.dram_tensor("wg_bf", [FF, D], BF16, kind="Internal").ap()
    wu_bf = nc.dram_tensor("wu_bf", [FF, D], BF16, kind="Internal").ap()
    wd_bf = nc.dram_tensor("wd_bf", [D, FF], BF16, kind="Internal").ap()
    wgT_bf = nc.dram_tensor("wgT_bf", [D, FF], BF16, kind="Internal").ap()
    wuT_bf = nc.dram_tensor("wuT_bf", [D, FF], BF16, kind="Internal").ap()
    wdT_bf = nc.dram_tensor("wdT_bf", [FF, D], BF16, kind="Internal").ap()
    skind = "ExternalOutput" if debug_scratch else "Internal"
    x1_s = nc.dram_tensor("x1_s", [NSEQ * S, D], F32, kind=skind).ap()
    ln3_s = nc.dram_tensor("ln3_s", [NSEQ * S, D], BF16, kind=skind).ap()

    es = contextlib.ExitStack()
    sb = lambda n, shp, d: es.enter_context(nc.sbuf_tensor(n, shp, d)).ap()

    ident_bf = sb("ident_bf", [128, 128], BF16)
    ones_bf = sb("ones_bf", [1, 128], BF16)
    ones64f = sb("ones64f", [1, 64], F32)
    bqT = sb("bqT", [128, 8], F32)
    bkT = sb("bkT", [128, 8], F32)

    bv_bf = sb("bv_bf", [1, D], BF16)
    bo_bf = sb("bo_bf", [1, D], BF16)
    brow = sb("brow", [1, D], F32)

    AR = sb("arena", [128, 49408], BF16)
    K_fm = AR[:, 0:16384]
    Q_fm = AR[:, 16384:32768]
    Vaug = AR[:, 32768:49408]
    fstage = AR[:, 0:24576]
    fstage2 = AR[:, 24576:49152]
    ln3_bl = AR[:, 0:8192]
    h_sfm = [AR[:, 8192:10240], AR[:, 10240:12288]]
    wgT_s = [AR[:, 12288:14336], AR[:, 14336:16384]]
    wuT_s = [AR[:, 16384:18432], AR[:, 18432:20480]]
    wdT_s = [AR[:, 20480:22528], AR[:, 22528:24576]]
    acc = AR[:, 24576:40960].bitcast(F32)
    o4 = AR[:, 40960:45056].bitcast(F32)

    ln1_bl = sb("ln1_bl", [128, 4096], BF16)
    Ebuf = sb("Ebuf", [128, 2048], BF16)
    attn_fm = sb("attn_fm", [128, 8192], BF16)
    srow = brow[0:1, 0:512]
    lsr = brow[0:1, 512:1024]
    rs = sb("rs", [1, 512], F32)
    rsb = sb("rsb", [64, 512], F32)
    o_tm = sb("o_tm", [128, 2048], F32)
    ident_f = o_tm[:, 256:384]
    bstat2 = o_tm[:, 384:396]
    mvt2 = o_tm[:, 396:398]
    rst2 = o_tm[:, 398:399]
    nmr2 = o_tm[:, 399:400]
    lvt2 = o_tm[:, 400:401]
    bstg = o_tm[0:8, 401:529]
    o_n = sb("o_n", [128, 1024], F32)
    xin = sb("xin", [128, 3072], F32)
    x1st = sb("x1st", [128, 2048], F32)
    ln3st = sb("ln3st", [128, 2048], BF16)
    ln1_tm = ln3st
    Wx = sb("Wx", [128, 16384], BF16)
    Wslot = [Wx[:, 0:4096], Wx[:, 4096:8192]]
    Wo_s = Wx[:, 8192:16384]
    gbf = sb("gbf", [128, 1024], BF16)
    rs_bf = gbf[0:1, 0:512]
    bstat = sb("bstat", [128, 12], F32)
    mvt = sb("mvt", [128, 2], F32)
    rst = sb("rst", [128, 1], F32)
    nmr = sb("nmr", [128, 1], F32)
    lvt = sb("lvt", [128, 1], F32)
    epsv = sb("epsv", [128, 1], F32)

    ps = es.enter_context(nc.psum_tensor("ps", [128, 4096], F32)).ap()
    ps_bf = ps.bitcast(BF16)

    def bank(b):
        return ps[:, 512 * b:512 * (b + 1)]

    def bank2(b):
        return ps[:, 512 * b:512 * (b + 2)]

    p = Plan()
    marks = {}
    rel = {}

    def rw(*names):
        return [rel[n] for n in names if n in rel]

    g_ld = p.sem("g_ld")
    g_ldg = p.sem("g_ldg")
    g_lds = p.sem("g_lds")
    g_st = p.sem("g_st")
    g_ev = p.sem("g_ev")
    g_mm = p.sem("g_mm")

    coli = o_tm[:, 0:128]
    p.step("gpsimd", lambda e: e.iota(ident_f, [[0, 128]], channel_multiplier=1,
                                      allow_small_or_imprecise_dtypes=True),
           inc=[(g_ev, 1)])
    p.step("gpsimd", lambda e: e.iota(coli, [[1, 128]], channel_multiplier=0,
                                      allow_small_or_imprecise_dtypes=True),
           inc=[(g_ev, 1)])
    p.step("vector", lambda e: e.tensor_tensor(ident_bf, ident_f, coli, ALU.is_equal),
           wait=[g_ev], inc=[(g_ev, 1)])
    p.step("vector", lambda e: e.tensor_tensor(ident_f, ident_f, coli, ALU.is_equal),
           wait=[g_ev], inc=[(g_ev, 1)])
    p.step("vector", lambda e: e.memset(ones_bf, 1.0), inc=[(g_ev, 1)])
    p.step("vector", lambda e: e.memset(ones64f, 1.0), inc=[(g_ev, 1)])
    p.step("vector", lambda e: e.memset(epsv, EPS), inc=[(g_ev, 1)])

    for nm, src, dstT in (("bq", BQ, bqT), ("bk", BK, bkT)):
        p.step("sync", lambda e, src=src: e.dma_start(
            out=bstg, in_=src.rearrange("(c p) -> c p", p=128)),
            wait=[(g_mm, g_mm.count)], inc=[(g_ld, 16)], tag=f"ldb_{nm}")
        p.step("tensor", lambda e: e.transpose(ps[:, 0:8], bstg, ident_f[0:8, 0:8]),
               wait=[g_ld, g_ev], inc=[(g_mm, 1)], tag=f"trb_{nm}")
        p.step("vector", lambda e, dstT=dstT: e.tensor_copy(dstT, ps[:, 0:8]),
               wait=[g_mm], inc=[(g_ev, 1)], tag=f"evb_{nm}")
    for nm, src, dst in (("bv", BV, bv_bf), ("bo", BO, bo_bf)):
        p.step("sync", lambda e, src=src: e.dma_start(
            out=brow, in_=src.rearrange("(a d) -> a d", a=1)),
            wait=[(g_ev, g_ev.count)], inc=[(g_ld, 16)], tag=f"ldr_{nm}")
        p.step("vector", lambda e, dst=dst: e.tensor_copy(dst, brow),
               wait=[g_ld], inc=[(g_ev, 1)], tag=f"cvr_{nm}")

    for nm, src in (("wq", WQ), ("wk", WK), ("wv", WV), ("wo", WO)):
        p.step("gpsimd", lambda e, src=src: e.dma_start(
            out=fstage[:, 0:8192], in_=src.rearrange("(c p) o -> p c o", p=128)),
            wait=[(g_st, g_st.count)], inc=[(g_ldg, 16)], tag=f"cast_{nm}")
        p.step("sync", lambda e, nm=nm: e.dma_start(
            out=wbf[nm].rearrange("(c p) o -> p c o", p=128), in_=fstage[:, 0:8192]),
            wait=[g_ldg], inc=[(g_st, 16)], tag=f"park_{nm}")
    for nm, src, dst in (("wg", WG, wg_bf), ("wu", WU, wu_bf), ("wd", WD, wd_bf)):
        p.step("gpsimd", lambda e, src=src: e.dma_start(
            out=fstage, in_=src.rearrange("(c p) o -> p c o", p=128)),
            wait=[(g_st, g_st.count)], inc=[(g_ldg, 16)], tag=f"cast_{nm}")
        p.step("sync", lambda e, dst=dst: e.dma_start(
            out=dst.rearrange("(c p) o -> p c o", p=128), in_=fstage),
            wait=[g_ldg], inc=[(g_st, 16)], tag=f"park_{nm}")

    for nm, srcs, dstT in (("wg", wg_bf, wgT_bf), ("wu", wu_bf, wuT_bf)):
        for c in range(8):
            for hg in range(6):
                p.step("scalar", lambda e, c=c, hg=hg, srcs=srcs: e.dma_start_transpose(
                    fstage2[:, c * 3072 + hg * 512: c * 3072 + (hg + 1) * 512],
                    srcs[hg * 512:(hg + 1) * 512, c * 128:(c + 1) * 128]),
                    wait=[(g_st, g_st.count)], inc=[(g_lds, 16)], tag=f"xp_{nm}{c}_{hg}")
        p.step("sync", lambda e, dstT=dstT: e.dma_start(
            out=dstT.rearrange("(c p) o -> p c o", p=128), in_=fstage2[:, 0:24576]),
            wait=[g_lds], inc=[(g_st, 16)], tag=f"parkT_{nm}")
    for hc in range(24):
        p.step("scalar", lambda e, hc=hc: e.dma_start_transpose(
            fstage2[:, hc * 1024:(hc + 1) * 1024], wd_bf[:, hc * 128:(hc + 1) * 128]),
            wait=[(g_st, g_st.count)], inc=[(g_lds, 16)], tag=f"xp_wd{hc}")
    p.step("sync", lambda e: e.dma_start(
        out=wdT_bf.rearrange("(c p) o -> p c o", p=128), in_=fstage2[:, 0:24576]),
        wait=[g_lds], inc=[(g_st, 16)], tag="parkT_wd")

    marks["prep"] = len(p.steps)
    prep_ev = (g_ev, g_ev.count)
    rel["arena"] = (g_st, g_st.count)
    rel["otm0"] = (g_ev, g_ev.count)
    rel["Wx0"] = (g_st, g_st.count)
    rel["Wx1"] = (g_st, g_st.count)
    rel["Wxo"] = (g_st, g_st.count)
    rel["banks03"] = (g_ev, g_ev.count)
    rel["bank6"] = prep_ev

    lxs = [p.sem("lxa"), p.sem("lxb"), p.sem("lxc")]
    vns = p.sem("vns")
    lxi = [p.sem("lxia"), p.sem("lxib")]
    cs_x1 = [p.sem("csx1a"), p.sem("csx1b")]
    cs_l3 = [p.sem("csl3a"), p.sem("csl3b")]
    st2 = [p.sem("sta"), p.sem("stb")]
    lw = [p.sem("lwa"), p.sem("lwb")]
    lwo = p.sem("lwo")
    prn = p.sem("prn")
    ptr = p.sem("ptr")
    pev = p.sem("pev")
    kqm = p.sem("kqm")
    kqe = p.sem("kqe")
    vev = p.sem("vev")
    sxp = p.sem("sxp")
    epi = p.sem("epi")
    orw = p.sem("orw")
    oev = p.sem("oev")
    ce = p.sem("ce")
    fw = [p.sem("fwa"), p.sem("fwb")]
    fmm = p.sem("fmm")
    fge = p.sem("fge")
    fdm = p.sem("fdm")
    fda = p.sem("fda")
    fln = p.sem("fln")
    fep = p.sem("fep")
    st_sems = st2
    for s in range(NSEQ):
        R0 = s * S
        smm = p.sem(f"smm{s}")
        pvm = p.sem(f"pvm{s}")

        def ln_stats(csem, src_ap, waits=(), alt=False):
            bs, mv = (bstat2, mvt2) if alt else (bstat, mvt)
            p.step("vector", lambda e, src_ap=src_ap, bs=bs: e.bn_stats(
                bs[:, 0:6], src_ap[:, 0:512]),
                wait=list(waits) + [csem], inc=[(csem, 1)])
            p.step("vector", lambda e, src_ap=src_ap, bs=bs: e.bn_stats(
                bs[:, 6:12], src_ap[:, 512:1024]),
                inc=[(csem, 1)])
            p.step("vector", lambda e, bs=bs, mv=mv: e.bn_aggr(
                mv, bs.rearrange("p (g x) -> p g x", x=6)),
                wait=[csem], inc=[(csem, 1)])

        def ln_scalar_chain(csem, alt=False):
            mv, rstt, nm, lv = ((mvt2, rst2, nmr2, lvt2) if alt
                                else (mvt, rst, nmr, lvt))
            p.step("scalar", lambda e, mv=mv, lv=lv: e.activation(
                lv, mv[:, 1:2], AF.Ln, bias=epsv),
                wait=[csem], inc=[(csem, 1)])
            p.step("scalar", lambda e, rstt=rstt, lv=lv: e.activation(
                rstt, lv, AF.Exp, scale=-0.5),
                wait=[csem], inc=[(csem, 1)])
            p.step("vector", lambda e, mv=mv, rstt=rstt, nm=nm: e.scalar_tensor_tensor(
                nm, mv[:, 0:1], -1.0, rstt, ALU.mult, ALU.mult),
                wait=[csem], inc=[(csem, 1)])

        p.step("vector", lambda e: e.memset(Vaug, 1.0),
               wait=rw("arena", "arenab"), inc=[(vns, 1)], tag=f"s{s}vones")
        kqi = 0
        vi = 0
        for g in range(4):
            for t4 in range(4):
                t = 4 * g + t4
                sl = t % 3
                xs = xin[:, sl * 1024:(sl + 1) * 1024]
                lt = ln1_tm[:, (t % 2) * 1024:(t % 2 + 1) * 1024]
                p.step("sync", lambda e, t=t, xs=xs, R0=R0: e.dma_start(
                    out=xs, in_=X[R0 + 128 * t: R0 + 128 * (t + 1), :]),
                    wait=rw(f"xin{sl}"), inc=[(lxs[sl], 16)], tag=f"s{s}ldx{t}")
                ln_stats(prn, xs, waits=[lxs[sl]], alt=True)
                ln_scalar_chain(prn, alt=True)
                p.step("vector", lambda e, xs=xs, lt=lt: e.scalar_tensor_tensor(
                    lt, xs, rst2, nmr2.broadcast_to([128, 1024]), ALU.mult, ALU.add),
                    wait=[prn] + rw(f"ln3st{t % 2}"), inc=[(prn, 1)], tag=f"s{s}nrm{t}")
                rel[f"xin{sl}"] = (prn, prn.count)
                for c in range(8):
                    bcol = (6 + c % 2) * 1024 + (c // 2) * 128
                    slot = ps_bf[:, bcol:bcol + 128]
                    w = [(pev, max(0, pev.count - 1))]
                    if c == 0:
                        w.append((prn, prn.count))
                        if t == 0:
                            w += rw("bank6") + rw("bank7")
                            if s == 0:
                                w.append(prep_ev)
                    p.step("tensor", lambda e, lt=lt, c=c, slot=slot: e.transpose(
                        slot, lt[:, c * 128:(c + 1) * 128], ident_bf),
                        wait=w, inc=[(ptr, 1)], tag=f"s{s}tr{t}_{c}")
                    p.step("vector", lambda e, c=c, t4=t4, slot=slot: e.tensor_copy(
                        ln1_bl[:, c * 512 + t4 * 128: c * 512 + (t4 + 1) * 128], slot),
                        wait=[ptr] + (rw("ln1bl") if c == 0 and t4 == 0 else []),
                        inc=[(pev, 1)], tag=f"s{s}ev{t}_{c}")
                marks[f"A{s}t{t}"] = len(p.steps)
            marks[f"A{s}g{g}p"] = len(p.steps)
            ldidx = 0
            for wnm, dest in (("wk", "K"), ("wq", "Q"), ("wv", "V")):
                for half in range(2):
                    wsl = (g * 6 + ldidx) % 2
                    ldidx += 1
                    lws = lw[wsl]
                    p.step("gpsimd", lambda e, wnm=wnm, half=half, wsl=wsl: e.dma_start(
                        out=Wslot[wsl],
                        in_=wbf[wnm][:, 512 * half:512 * (half + 1)]
                        .rearrange("(c p) o -> p c o", p=128)),
                        wait=rw(f"Wx{wsl}"), inc=[(lws, 16)], tag=f"s{s}lw{g}{wnm}{half}")
                    if dest in ("K", "Q"):
                        out_fm = K_fm if dest == "K" else Q_fm
                        bT = bkT if dest == "K" else bqT
                        for do4 in range(4):
                            DO = 4 * half + do4
                            bk_i = 4 + (kqi % 2)
                            for di in range(8):
                                w = []
                                if di == 0:
                                    w = [(lws, lws.count), (pev, pev.count),
                                         (kqe, max(0, kqe.count - 1))]
                                    if kqi == 0:
                                        w += rw("banks45")
                                        if s == 0:
                                            w.append(prep_ev)
                                p.step("tensor", lambda e, wsl=wsl, di=di, do4=do4, bk_i=bk_i: e.matmul(
                                    bank(bk_i),
                                    Wslot[wsl][:, di * 512 + do4 * 128: di * 512 + (do4 + 1) * 128],
                                    ln1_bl[:, di * 512:(di + 1) * 512],
                                    start=(di == 0), stop=(di == 7)),
                                    wait=w, inc=([(kqm, 1)] if di == 7 else []),
                                    tag=f"s{s}b{dest}{g}{half}{do4}_{di}")
                            p.step("vector", lambda e, out_fm=out_fm, DO=DO, bT=bT, bk_i=bk_i, g=g: e.tensor_scalar(
                                out_fm[:, DO * 2048 + g * 512: DO * 2048 + (g + 1) * 512],
                                bank(bk_i), bT[:, DO:DO + 1], None, ALU.add),
                                wait=[kqm] + (rw("arena", "arenab") if kqi == 0 else []),
                                inc=[(kqe, 1)], tag=f"s{s}e{dest}{g}{half}{do4}")
                            kqi += 1
                    else:
                        for t4 in range(4):
                            kt = 4 * g + t4
                            bk_i = vi % 4
                            for di in range(8):
                                w = []
                                if di == 0:
                                    w = [(lws, lws.count), (pev, pev.count),
                                         (vev, max(0, vev.count - 3))]
                                    if vi == 0:
                                        w += rw("banks03")
                                p.step("tensor", lambda e, wsl=wsl, di=di, t4=t4, bk_i=bk_i: e.matmul(
                                    bank(bk_i),
                                    ln1_bl[:, di * 512 + t4 * 128: di * 512 + (t4 + 1) * 128],
                                    Wslot[wsl][:, di * 512:(di + 1) * 512],
                                    start=(di == 0), stop=False),
                                    wait=w, tag=f"s{s}bV{g}{half}{t4}_{di}")
                            p.step("tensor", lambda e, half=half, bk_i=bk_i: e.matmul(
                                bank(bk_i), ones_bf[0:1, :],
                                bv_bf[0:1, 512 * half:512 * (half + 1)],
                                start=False, stop=True),
                                inc=[(kqm, 1)], tag=f"s{s}bVb{g}{half}{t4}")
                            for hh in range(8):
                                hgl = 8 * half + hh
                                p.step("vector", lambda e, kt=kt, hgl=hgl, hh=hh, bk_i=bk_i: e.tensor_copy(
                                    Vaug[:, kt * 1040 + hgl * 65: kt * 1040 + hgl * 65 + 64],
                                    ps[:, 512 * bk_i + hh * 64: 512 * bk_i + (hh + 1) * 64]),
                                    wait=([kqm] if hh == 0 else []),
                                    inc=([(vev, 1)] if hh == 7 else []),
                                    tag=f"s{s}eV{g}{half}{t4}_{hh}")
                            vi += 1
                    rel[f"Wx{wsl}"] = (kqm, kqm.count)
            rel["ln1bl"] = (kqm, kqm.count)
            marks[f"A{s}g{g}"] = len(p.steps)
        p.step("gpsimd", lambda e: e.dma_start(
            out=Wo_s, in_=wbf["wo"].rearrange("(c p) o -> p c o", p=128)),
            wait=rw("Wxo"), inc=[(lwo, 16)], tag=f"s{s}lwo")

        marks[f"A{s}"] = len(p.steps)
        builds_done = [(kqe, kqe.count), (vev, vev.count), (kqm, kqm.count),
                       (vns, vns.count)]

        def plan_pv_pair(u, jp, sxp_val):
            h = u % 16
            pr = jp % 2
            for j in (2 * jp, 2 * jp + 1):
                wpv = [(sxp, sxp_val)]
                if j == 0:
                    wpv.append((epi, epi.count))
                p.step("tensor", lambda e, j=j, pr=pr, u=u, h=h: e.matmul(
                    ps[0:65, 2048 + 512 * (u % 2): 2048 + 512 * (u % 2) + 512],
                    Vaug[:, j * 1040 + h * 65: j * 1040 + h * 65 + 65],
                    Ebuf[:, 1024 * pr + 512 * (j % 2): 1024 * pr + 512 * (j % 2) + 512],
                    start=(j == 0), stop=(j == 15)),
                    wait=wpv, inc=([(pvm, 1)] if j % 2 == 1 else []),
                    tag=f"s{s}pv{u}_{j}")

        def plan_unit(u):
            g, h = u // 16, u % 16
            hc, hr = h // 2, 64 * (h % 2)
            for jp in range(8):
                P = u * 8 + jp
                pr = P % 2
                for j in (2 * jp, 2 * jp + 1):
                    w = []
                    if j % 2 == 0:
                        w.append((sxp, max(0, sxp.count - 1)))
                    if u == 0 and j == 0:
                        w += builds_done
                    p.step("tensor", lambda e, j=j, pr=pr, hc=hc, hr=hr, g=g: e.matmul(
                        ps[:, 1024 * pr + 512 * (j % 2): 1024 * pr + 512 * (j % 2) + 512],
                        K_fm[hr:hr + 64, hc * 2048 + j * 128: hc * 2048 + (j + 1) * 128],
                        Q_fm[hr:hr + 64, hc * 2048 + g * 512: hc * 2048 + (g + 1) * 512],
                        start=True, stop=True),
                        wait=w, inc=([(smm, 1)] if j % 2 == 1 else []),
                        tag=f"s{s}sT{u}_{j}")
                p.step("scalar", lambda e, pr=pr: e.activation(
                    Ebuf[:, 1024 * pr:1024 * (pr + 1)],
                    ps[:, 1024 * pr:1024 * (pr + 1)], AF.Exp, scale=0.125),
                    wait=[(smm, smm.count), (pvm, pvm.count)],
                    inc=[(sxp, 1)], tag=f"s{s}xp{u}_{P}")
                if PVLAG:
                    if jp >= 1:
                        plan_pv_pair(u, jp - 1, sxp.count - 1)
                else:
                    plan_pv_pair(u, jp, sxp.count)
            if PVLAG:
                plan_pv_pair(u, 7, sxp.count)

        def plan_epi1(u):
            p.step("vector", lambda e, u=u: e.tensor_copy(
                srow, ps[64:65, 2048 + 512 * (u % 2): 2048 + 512 * (u % 2) + 512]),
                wait=[(pvm, pvm.count), (epi, epi.count)],
                inc=[(epi, 1)], tag=f"s{s}e1_{u}")

        def plan_epi2(u):
            g, h = u // 16, u % 16
            hc, hr = h // 2, 64 * (h % 2)
            gb = (g % 2) * 4096
            p.step("scalar", lambda e: e.activation(lsr, srow, AF.Ln),
                   wait=[epi], inc=[(epi, 1)], tag=f"s{s}e2_{u}")
            p.step("scalar", lambda e: e.activation(rs, lsr, AF.Exp, scale=-1.0),
                   wait=[epi], inc=[(epi, 1)], tag=f"s{s}e3_{u}")
            if E4BF16:
                p.step("vector", lambda e: e.tensor_copy(rs_bf, rs),
                       wait=[epi], inc=[(epi, 1)], tag=f"s{s}e3b_{u}")
                p.step("tensor", lambda e: e.matmul(
                    ps[0:64, 3072:3584], ones_bf[0:1, 0:64], rs_bf[0:1, :],
                    start=True, stop=True),
                    wait=[epi], inc=[(epi, 1)], tag=f"s{s}e4_{u}")
            else:
                p.step("tensor", lambda e: e.matmul(
                    ps[0:64, 3072:3584], ones64f[0:1, :], rs[0:1, :],
                    start=True, stop=True),
                    wait=[epi], inc=[(epi, 1)], tag=f"s{s}e4_{u}")
            p.step("vector", lambda e: e.tensor_copy(rsb, ps[0:64, 3072:3584]),
                   wait=[epi], inc=[(epi, 1)], tag=f"s{s}e5_{u}")
            p.step("vector", lambda e, u=u, hr=hr, hc=hc, gb=gb: e.scalar_tensor_tensor(
                attn_fm[hr:hr + 64, gb + hc * 512: gb + (hc + 1) * 512],
                ps[0:64, 2048 + 512 * (u % 2): 2048 + 512 * (u % 2) + 512],
                1.0, rsb, ALU.mult, ALU.mult),
                wait=[epi, (orw, 8 * max(0, u // 16 - 1))],
                inc=[(epi, 1)], tag=f"s{s}e6_{u}")

        def plan_otile(g, t4, nlist=(0, 1)):
            gb = (g % 2) * 4096
            t = 4 * g + t4
            sl = t % 2
            for n in nlist:
                for di in range(8):
                    w = []
                    if di == 0:
                        w = [(epi, epi_group_done[g]), (lwo, lwo.count),
                             (oev, oev.count)]
                        if g == 0 and t4 == 0 and n == 0:
                            w += rw("bank7")
                    p.step("tensor", lambda e, di=di, t4=t4, n=n, gb=gb: e.matmul(
                        bank(7),
                        attn_fm[:, gb + di * 512 + t4 * 128: gb + di * 512 + (t4 + 1) * 128],
                        Wo_s[:, di * 1024 + n * 512: di * 1024 + (n + 1) * 512],
                        start=(di == 0), stop=False),
                        wait=w, tag=f"s{s}op{g}{t4}{n}_{di}")
                p.step("tensor", lambda e, n=n: e.matmul(
                    bank(7), ones_bf[0:1, :], bo_bf[0:1, n * 512:(n + 1) * 512],
                    start=False, stop=True),
                    inc=[(orw, 1)], tag=f"s{s}opb{g}{t4}{n}")
                p.step("scalar", lambda e, sl=sl, n=n: e.activation(
                    o_tm[:, sl * 1024 + n * 512: sl * 1024 + (n + 1) * 512],
                    bank(7), AF.Copy),
                    wait=[(orw, orw.count)] + (rw(f"otm{sl}") if n == 0 else []),
                    inc=[(oev, 1)], tag=f"s{s}oe{g}{t4}{n}")

        def plan_ctile(g, t4):
            t = 4 * g + t4
            sl = t % 2
            xsl = t % 3
            ot = o_tm[:, sl * 1024:(sl + 1) * 1024]
            xs = xin[:, xsl * 1024:(xsl + 1) * 1024]
            xv = x1st[:, sl * 1024:(sl + 1) * 1024]
            lv = ln3st[:, sl * 1024:(sl + 1) * 1024]
            p.step("sync", lambda e, t=t, xs=xs, R0=R0: e.dma_start(
                out=xs, in_=X[R0 + 128 * t: R0 + 128 * (t + 1), :]),
                wait=rw(f"xin{xsl}"), inc=[(lxs[xsl], 16)], tag=f"s{s}rx{t}")
            ln_stats(ce, ot, waits=[(oev, oev.count)] + rw("ln_scalars"))
            ln_scalar_chain(ce)
            p.step("vector", lambda e, ot=ot: e.scalar_tensor_tensor(
                o_n, ot, rst, nmr.broadcast_to([128, 1024]), ALU.mult, ALU.add),
                wait=[ce] + rw("o_n"), inc=[(ce, 1)], tag=f"s{s}on{t}")
            rel[f"otm{sl}"] = (ce, ce.count)
            rel["ln_scalars"] = (ce, ce.count)
            p.step("vector", lambda e, xv=xv, xs=xs: e.tensor_add(xv, o_n, xs),
                   wait=[ce, (lxs[xsl], lxs[xsl].count)] + rw(f"x1st{sl}"),
                   inc=[(ce, 1)], tag=f"s{s}x1{t}")
            rel["o_n"] = (ce, ce.count)
            rel[f"xin{xsl}"] = (ce, ce.count)
            ln_stats(ce, xv)
            ln_scalar_chain(ce)
            p.step("vector", lambda e, xv=xv, lv=lv: e.scalar_tensor_tensor(
                lv, xv, rst, nmr.broadcast_to([128, 1024]), ALU.mult, ALU.add),
                wait=[ce] + rw(f"ln3st{sl}"), inc=[(ce, 1)], tag=f"s{s}l3{t}")
            rel["ln_scalars"] = (ce, ce.count)
            p.step("sync", lambda e, t=t, xv=xv, R0=R0: e.dma_start(
                out=x1_s[R0 + 128 * t: R0 + 128 * (t + 1), :], in_=xv),
                wait=[(ce, ce.count)], inc=[(cs_x1[sl], 16)], tag=f"s{s}sx1{t}")
            rel[f"x1st{sl}"] = (cs_x1[sl], cs_x1[sl].count)
            p.step("sync", lambda e, t=t, lv=lv, R0=R0: e.dma_start(
                out=ln3_s[R0 + 128 * t: R0 + 128 * (t + 1), :], in_=lv),
                inc=[(cs_l3[sl], 16)], tag=f"s{s}sl3{t}")
            rel[f"ln3st{sl}"] = (cs_l3[sl], cs_l3[sl].count)

        ln3_ready = {}
        epi_group_done = {}
        for u in range(65):
            if u < 64:
                plan_unit(u)
            if u >= 1:
                plan_epi2(u - 1)
                if (u - 1) % 16 == 15:
                    epi_group_done[(u - 1) // 16] = epi.count
            if u < 64:
                plan_epi1(u)
            gp = u // 16 - 1
            idx = u % 16
            if SPREAD_OPROJ:
                if gp >= 0 and 1 <= idx <= 8:
                    t4, n = (idx - 1) // 2, (idx - 1) % 2
                    plan_otile(gp, t4, nlist=(n,))
                    if n == 1:
                        plan_ctile(gp, t4)
                        if t4 == 3:
                            ln3_ready[gp] = [(cs_l3[0], cs_l3[0].count),
                                             (cs_l3[1], cs_l3[1].count)]
                if u == 64:
                    for t4 in range(4):
                        plan_otile(3, t4)
                        plan_ctile(3, t4)
                    ln3_ready[3] = [(cs_l3[0], cs_l3[0].count),
                                    (cs_l3[1], cs_l3[1].count)]
            else:
                if u >= 1 and (u - 1) % 16 == 15:
                    g = (u - 1) // 16
                    for t4 in range(4):
                        plan_otile(g, t4)
                        plan_ctile(g, t4)
                    ln3_ready[g] = [(cs_l3[0], cs_l3[0].count),
                                    (cs_l3[1], cs_l3[1].count)]
        marks[f"B{s}"] = len(p.steps)
        attn_done = [(sxp, sxp.count), (epi, epi.count), (oev, oev.count),
                     (orw, orw.count), (smm, smm.count)]

        rel["fw0"] = (smm, smm.count)
        rel["fw1"] = (smm, smm.count)
        rel["ln3bl"] = (smm, smm.count)
        rel["accH"] = (epi, epi.count)

        hsg = 0
        ggi = 0
        for Hh in range(2):
            for u2 in range(2):
                for di in range(8):
                    w = list(ln3_ready[2 * Hh + u2])
                    if u2 == 0 and di == 0:
                        w += rw("ln3bl")
                        w.append((fmm, fmm.count))
                    p.step("sync", lambda e, Hh=Hh, u2=u2, di=di, R0=R0: e.dma_start_transpose(
                        ln3_bl[:, di * 1024 + u2 * 512: di * 1024 + (u2 + 1) * 512],
                        ln3_s[R0 + 1024 * Hh + 512 * u2: R0 + 1024 * Hh + 512 * (u2 + 1),
                              di * 128:(di + 1) * 128]),
                        wait=w, inc=[(fln, 16)], tag=f"s{s}fl{Hh}{u2}_{di}")
            rel["ln1bl"] = (fln, fln.count)
            fln_done = fln.count
            fge_done = {}
            down_pending = []

            def plan_down(job):
                hs, hsl, fsl = job
                for tt in range(8):
                    dbk = 4 + 2 * (tt % 2)
                    at = acc[:, 1024 * tt:1024 * (tt + 1)]
                    for c in range(2):
                        for n in range(2):
                            w = []
                            if c == 0 and n == 0:
                                w = [(fge, fge_done[hs]),
                                     (fda, max(0, fda.count - 1))]
                                if hs == 0 and tt < 2:
                                    w.append((fln, fln_done))
                                    if Hh == 0:
                                        w += attn_done
                            p.step("tensor", lambda e, c=c, n=n, tt=tt, hsl=hsl, fsl=fsl, dbk=dbk: e.matmul(
                                bank(dbk + n),
                                h_sfm[hsl][:, c * 1024 + tt * 128: c * 1024 + (tt + 1) * 128],
                                wdT_s[fsl][:, c * 1024 + n * 512: c * 1024 + (n + 1) * 512],
                                start=(c == 0), stop=(c == 1)),
                                wait=w,
                                inc=([(fdm, 1)] if c == 1 and n == 1 else []),
                                tag=f"s{s}d{Hh}{hs}{tt}_{c}{n}")
                    if hs == 0:
                        p.step("vector", lambda e, at=at, dbk=dbk: e.tensor_scalar(
                            at, bank2(dbk), QSC, None, ALU.mult),
                            wait=[(fdm, fdm.count)] + (rw("accH") if tt == 0 and Hh == 0 else []),
                            inc=[(fda, 1)], tag=f"s{s}da{Hh}{hs}{tt}")
                    else:
                        p.step("vector", lambda e, at=at, dbk=dbk: e.scalar_tensor_tensor(
                            at, bank2(dbk), QSC, at, ALU.mult, ALU.add),
                            wait=[(fdm, fdm.count)],
                            inc=[(fda, 1)], tag=f"s{s}da{Hh}{hs}{tt}")
                rel[f"fw{fsl}"] = (fdm, fdm.count)

            fln_done = fln.count
            for hs in range(12):
                fsl = hsg % 2
                hof = hs * 256
                fws = fw[fsl]
                for wsrc, wdst in ((wgT_bf, wgT_s), (wuT_bf, wuT_s)):
                    p.step("gpsimd", lambda e, wsrc=wsrc, wdst=wdst, fsl=fsl, hof=hof: e.dma_start(
                        out=wdst[fsl],
                        in_=wsrc[:, hof:hof + 256].rearrange("(c p) h -> p c h", p=128)),
                        wait=rw(f"fw{fsl}"), inc=[(fws, 16)], tag=f"s{s}fw{Hh}{hs}")
                p.step("gpsimd", lambda e, fsl=fsl, hof=hof: e.dma_start(
                    out=wdT_s[fsl],
                    in_=wdT_bf[hof:hof + 256, :].rearrange("(c p) d -> p c d", p=128)),
                    inc=[(fws, 16)], tag=f"s{s}fwd{Hh}{hs}")
                hsl = hsg % 2
                for u2 in range(2):
                    for c in range(2):
                        gbk = ggi % 2
                        ubk = 2 + ggi % 2
                        gsl = ggi % 2
                        for di in range(8):
                            w = []
                            if di == 0:
                                w = [(fws, fws.count), (fln, fln.count),
                                     (fge, max(0, fge.count - 2))]
                                if Hh == 0 and hs == 0 and u2 == 0 and c == 0:
                                    w += attn_done
                            p.step("tensor", lambda e, di=di, c=c, u2=u2, fsl=fsl, gbk=gbk: e.matmul(
                                bank(gbk),
                                wgT_s[fsl][:, di * 256 + c * 128: di * 256 + (c + 1) * 128],
                                ln3_bl[:, di * 1024 + u2 * 512: di * 1024 + (u2 + 1) * 512],
                                start=(di == 0), stop=(di == 7)),
                                wait=w, tag=f"s{s}g{Hh}{hs}{u2}{c}_{di}")
                            p.step("tensor", lambda e, di=di, c=c, u2=u2, fsl=fsl, ubk=ubk: e.matmul(
                                bank(ubk),
                                wuT_s[fsl][:, di * 256 + c * 128: di * 256 + (c + 1) * 128],
                                ln3_bl[:, di * 1024 + u2 * 512: di * 1024 + (u2 + 1) * 512],
                                start=(di == 0), stop=(di == 7)),
                                inc=([(fmm, 1)] if di == 7 else []),
                                tag=f"s{s}uu{Hh}{hs}{u2}{c}_{di}")
                        p.step("scalar", lambda e, gbk=gbk, gsl=gsl: e.activation(
                            gbf[:, gsl * 512:(gsl + 1) * 512], bank(gbk), AF.Silu,
                            scale=QSC),
                            wait=[(fmm, fmm.count), (fge, fge.count)],
                            inc=[(fge, 1)], tag=f"s{s}sl{Hh}{hs}{u2}{c}")
                        p.step("vector", lambda e, ubk=ubk, gsl=gsl, hsl=hsl, c=c, u2=u2: e.scalar_tensor_tensor(
                            h_sfm[hsl][:, c * 1024 + u2 * 512: c * 1024 + (u2 + 1) * 512],
                            bank(ubk), QSC, gbf[:, gsl * 512:(gsl + 1) * 512],
                            ALU.mult, ALU.mult),
                            wait=[(fge, fge.count), (fdm, fdm.count)],
                            inc=[(fge, 1)], tag=f"s{s}h{Hh}{hs}{u2}{c}")
                        ggi += 1
                fge_done[hs] = fge.count
                down_pending.append((hs, hsl, fsl))
                if hs >= 1:
                    plan_down(down_pending.pop(0))
                if hs == 11:
                    plan_down(down_pending.pop(0))
                hsg += 1
            for tt in range(8):
                t = 8 * Hh + tt
                sl = t % 2
                at = acc[:, 1024 * tt:1024 * (tt + 1)]
                ov = o4[:, sl * 1024:(sl + 1) * 1024]
                xs = x1st[:, sl * 1024:(sl + 1) * 1024]
                p.step("sync", lambda e, t=t, xs=xs, R0=R0: e.dma_start(
                    out=xs, in_=x1_s[R0 + 128 * t: R0 + 128 * (t + 1), :]),
                    wait=rw(f"x1st{sl}") + [(cs_x1[sl], cs_x1[sl].count)],
                    inc=[(lxi[sl], 16)], tag=f"s{s}fx1{t}")
                ln_stats(fep, at, waits=[(fda, fda.count)] + rw("ln_scalars"))
                ln_scalar_chain(fep)
                p.step("vector", lambda e, at=at, ov=ov: e.scalar_tensor_tensor(
                    ov, at, rst, nmr.broadcast_to([128, 1024]), ALU.mult, ALU.add),
                    wait=[fep] + rw(f"o4{sl}"), inc=[(fep, 1)], tag=f"s{s}f4{t}")
                rel["ln_scalars"] = (fep, fep.count)
                p.step("vector", lambda e, ov=ov, xs=xs: e.tensor_add(ov, ov, xs),
                       wait=[fep, (lxi[sl], lxi[sl].count)], inc=[(fep, 1)],
                       tag=f"s{s}f5{t}")
                rel[f"x1st{sl}"] = (fep, fep.count)
                p.step("sync", lambda e, t=t, ov=ov, R0=R0: e.dma_start(
                    out=OUT[R0 + 128 * t: R0 + 128 * (t + 1), :], in_=ov),
                    wait=[(fep, fep.count)], inc=[(st2[sl], 16)], tag=f"s{s}out{t}")
                rel[f"o4{sl}"] = (st2[sl], st2[sl].count)
            rel["accH"] = (fep, fep.count)

        marks[f"D{s}"] = len(p.steps)
        rel["arena"] = (st2[1], st2[1].count)
        rel["arenab"] = (st2[0], st2[0].count)
        rel["bank6"] = (fda, fda.count)
        rel["bank7"] = (fda, fda.count)
        rel["banks03"] = (fge, fge.count)
        rel["banks45"] = (fda, fda.count)
        rel["Wxo"] = (orw, orw.count)

    if truncate_at is not None:
        p.steps = p.steps[:truncate_at]
    elif truncate is not None:
        p.steps = p.steps[:marks[truncate]]
    else:
        p.step("sync", lambda e: e.nop(),
               wait=[(stx, stx.count) for stx in st_sems], tag="final")

    build_program.marks = dict(marks)
    build_program.plan = p
    p.emit(nc, es)
    es.close()
    return nc


_NC_CACHE = None


def kernel(L0_attn_x, L0_attn_Wq, L0_attn_bq, L0_attn_Wk, L0_attn_bk,
           L0_attn_Wv, L0_attn_bv, L0_attn_Wo, L0_attn_bo,
           L0_ffn_w_gate, L0_ffn_w_up, L0_ffn_w_down):
    global _NC_CACHE
    if _NC_CACHE is None:
        _NC_CACHE = build_program()
    nc = _NC_CACHE

    x = np.ascontiguousarray(np.asarray(L0_attn_x, dtype=np.float32))
    base = dict(
        wq=np.ascontiguousarray(np.asarray(L0_attn_Wq, dtype=np.float32)),
        wk=np.ascontiguousarray(np.asarray(L0_attn_Wk, dtype=np.float32)),
        wv=np.ascontiguousarray(np.asarray(L0_attn_Wv, dtype=np.float32)),
        wo=np.ascontiguousarray(np.asarray(L0_attn_Wo, dtype=np.float32)),
        bq=np.ascontiguousarray(np.asarray(L0_attn_bq, dtype=np.float32)),
        bk=np.ascontiguousarray(np.asarray(L0_attn_bk, dtype=np.float32)),
        bv=np.ascontiguousarray(np.asarray(L0_attn_bv, dtype=np.float32)),
        bo=np.ascontiguousarray(np.asarray(L0_attn_bo, dtype=np.float32)),
        wgi=np.ascontiguousarray(np.asarray(L0_ffn_w_gate, dtype=np.int32)),
        wui=np.ascontiguousarray(np.asarray(L0_ffn_w_up, dtype=np.int32)),
        wdi=np.ascontiguousarray(np.asarray(L0_ffn_w_down, dtype=np.int32)),
    )
    in_maps = []
    for i in range(8):
        m = dict(base)
        m["x"] = np.ascontiguousarray(
            x[NSEQ * i:NSEQ * (i + 1)].reshape(NSEQ * S, D))
        in_maps.append(m)

    res = run_bass_kernel_spmd(nc, in_maps, core_ids=list(range(8)))
    outs = [np.asarray(r["out"]).reshape(NSEQ, S, D) for r in res.results]
    return np.concatenate(outs, axis=0).astype(np.float32)


# version 07
# speedup vs baseline: 1.0198x; 1.0198x over previous
import contextlib
import numpy as np
import concourse.bass as bass
import concourse.mybir as mybir
from concourse.bass_utils import run_bass_kernel_spmd

F32 = mybir.dt.float32
BF16 = mybir.dt.bfloat16
I32 = mybir.dt.int32
AF = mybir.ActivationFunctionType
ALU = mybir.AluOpType

D = 1024
S = 2048
NSEQ = 4
FF = 3072
EPS = 1e-5
QSC = 1.0 / 127.0
SPREAD_OPROJ = True
PVLAG = True
E4BF16 = False


class PSem:
    __slots__ = ("name", "count", "handle")

    def __init__(self, name):
        self.name = name
        self.count = 0
        self.handle = None


class Plan:
    def __init__(self):
        self.steps = []
        self.sems = []

    def sem(self, name):
        s = PSem(name)
        self.sems.append(s)
        return s

    def step(self, engine, fn, wait=(), inc=(), tag=""):
        waits = []
        for w in wait:
            if w is None:
                continue
            if isinstance(w, PSem):
                waits.append((w, w.count))
            else:
                waits.append(w)
        incs = list(inc)
        for sm, amt in incs:
            sm.count += amt
        self.steps.append((engine, fn, waits, incs, tag))

    def emit(self, nc, es):
        for sm in self.sems:
            sm.handle = es.enter_context(nc.semaphore(sm.name))
        block = es.enter_context(nc.Block())
        steps = self.steps

        self.emitted = [None] * len(steps)

        def mk(engine_name):
            def body(eng):
                waited = {}
                for si, (engine, fn, waits, incs, tag) in enumerate(steps):
                    if engine != engine_name:
                        continue
                    for sm, val in waits:
                        if val > 0 and waited.get(id(sm), -1) < val:
                            eng.wait_ge(sm.handle, val)
                            waited[id(sm)] = val
                    inst = fn(eng)
                    self.emitted[si] = inst
                    if incs:
                        assert inst is not None, f"step {tag} needs inst for incs"
                        for sm, amt in incs:
                            inst = inst.then_inc(sm.handle, amt)
            return body

        used = {st[0] for st in steps}
        for name, opener in (("sync", block.sync), ("tensor", block.tensor),
                             ("vector", block.vector), ("scalar", block.scalar),
                             ("gpsimd", block.gpsimd)):
            if name in used:
                opener(mk(name))


def build_program(truncate=None, truncate_at=None, debug_scratch=False):
    nc = bass.Bass("TRN2", target_bir_lowering=False)

    X = nc.dram_tensor("x", [NSEQ * S, D], F32, kind="ExternalInput").ap()
    OUT = nc.dram_tensor("out", [NSEQ * S, D], F32, kind="ExternalOutput").ap()
    WQ = nc.dram_tensor("wq", [D, D], F32, kind="ExternalInput").ap()
    WK = nc.dram_tensor("wk", [D, D], F32, kind="ExternalInput").ap()
    WV = nc.dram_tensor("wv", [D, D], F32, kind="ExternalInput").ap()
    WO = nc.dram_tensor("wo", [D, D], F32, kind="ExternalInput").ap()
    BQ = nc.dram_tensor("bq", [D], F32, kind="ExternalInput").ap()
    BK = nc.dram_tensor("bk", [D], F32, kind="ExternalInput").ap()
    BV = nc.dram_tensor("bv", [D], F32, kind="ExternalInput").ap()
    BO = nc.dram_tensor("bo", [D], F32, kind="ExternalInput").ap()
    WG = nc.dram_tensor("wgi", [FF, D], I32, kind="ExternalInput").ap()
    WU = nc.dram_tensor("wui", [FF, D], I32, kind="ExternalInput").ap()
    WD = nc.dram_tensor("wdi", [D, FF], I32, kind="ExternalInput").ap()

    wbf = {}
    for nm in ("wq", "wk", "wv", "wo"):
        wbf[nm] = nc.dram_tensor(nm + "_bf", [D, D], BF16, kind="Internal").ap()
    wg_bf = nc.dram_tensor("wg_bf", [FF, D], BF16, kind="Internal").ap()
    wu_bf = nc.dram_tensor("wu_bf", [FF, D], BF16, kind="Internal").ap()
    wd_bf = nc.dram_tensor("wd_bf", [D, FF], BF16, kind="Internal").ap()
    wgT_bf = nc.dram_tensor("wgT_bf", [D, FF], BF16, kind="Internal").ap()
    wuT_bf = nc.dram_tensor("wuT_bf", [D, FF], BF16, kind="Internal").ap()
    wdT_bf = nc.dram_tensor("wdT_bf", [FF, D], BF16, kind="Internal").ap()
    skind = "ExternalOutput" if debug_scratch else "Internal"
    x1_s = nc.dram_tensor("x1_s", [NSEQ * S, D], F32, kind=skind).ap()
    ln3_s = nc.dram_tensor("ln3_s", [NSEQ * S, D], BF16, kind=skind).ap()

    es = contextlib.ExitStack()
    sb = lambda n, shp, d: es.enter_context(nc.sbuf_tensor(n, shp, d)).ap()

    ident_bf = sb("ident_bf", [128, 128], BF16)
    ones_bf = sb("ones_bf", [1, 128], BF16)
    ones64f = sb("ones64f", [1, 64], F32)
    bqT = sb("bqT", [128, 8], F32)
    bkT = sb("bkT", [128, 8], F32)

    bv_bf = sb("bv_bf", [1, D], BF16)
    bo_bf = sb("bo_bf", [1, D], BF16)
    brow = sb("brow", [1, D], F32)

    AR = sb("arena", [128, 49408], BF16)
    K_fm = AR[:, 0:16384]
    Q_fm = AR[:, 16384:32768]
    Vaug = AR[:, 32768:49408]
    fstage = AR[:, 0:24576]
    fstage2 = AR[:, 24576:49152]
    ln3_bl = AR[:, 0:8192]
    h_sfm = [AR[:, 8192:10240], AR[:, 10240:12288]]
    wgT_s = [AR[:, 12288:14336], AR[:, 14336:16384]]
    wuT_s = [AR[:, 16384:18432], AR[:, 18432:20480]]
    wdT_s = [AR[:, 20480:22528], AR[:, 22528:24576]]
    acc = AR[:, 24576:40960].bitcast(F32)
    o4 = AR[:, 40960:45056].bitcast(F32)

    ln1_bl = sb("ln1_bl", [128, 4096], BF16)
    Ebuf = sb("Ebuf", [128, 2048], BF16)
    attn_fm = sb("attn_fm", [128, 8192], BF16)
    srow = brow[0:1, 0:512]
    lsr = brow[0:1, 512:1024]
    rs = sb("rs", [1, 512], F32)
    rsb = sb("rsb", [64, 512], F32)
    o_tm = sb("o_tm", [128, 2048], F32)
    ident_f = o_tm[:, 256:384]
    bstat2 = o_tm[:, 384:396]
    mvt2 = o_tm[:, 396:398]
    rst2 = o_tm[:, 398:399]
    nmr2 = o_tm[:, 399:400]
    lvt2 = o_tm[:, 400:401]
    bstg = o_tm[0:8, 401:529]
    o_n = sb("o_n", [128, 1024], F32)
    xin = sb("xin", [128, 3072], F32)
    x1st = sb("x1st", [128, 2048], F32)
    ln3st = sb("ln3st", [128, 2048], BF16)
    ln1_tm = ln3st
    Wx = sb("Wx", [128, 16384], BF16)
    Wslot = [Wx[:, 0:4096], Wx[:, 4096:8192]]
    Wo_s = Wx[:, 8192:16384]
    gbf = sb("gbf", [128, 1024], BF16)
    rs_bf = gbf[0:1, 0:512]
    Q_z = gbf
    bstat = sb("bstat", [128, 12], F32)
    mvt = sb("mvt", [128, 2], F32)
    rst = sb("rst", [128, 1], F32)
    nmr = sb("nmr", [128, 1], F32)
    lvt = sb("lvt", [128, 1], F32)
    epsv = sb("epsv", [128, 1], F32)

    ps = es.enter_context(nc.psum_tensor("ps", [128, 4096], F32)).ap()
    ps_bf = ps.bitcast(BF16)

    def bank(b):
        return ps[:, 512 * b:512 * (b + 1)]

    def bank2(b):
        return ps[:, 512 * b:512 * (b + 2)]

    p = Plan()
    marks = {}
    rel = {}

    def rw(*names):
        return [rel[n] for n in names if n in rel]

    g_ld = p.sem("g_ld")
    g_ldg = p.sem("g_ldg")
    g_lds = p.sem("g_lds")
    g_st = p.sem("g_st")
    g_ev = p.sem("g_ev")
    g_mm = p.sem("g_mm")

    coli = o_tm[:, 0:128]
    p.step("gpsimd", lambda e: e.iota(ident_f, [[0, 128]], channel_multiplier=1,
                                      allow_small_or_imprecise_dtypes=True),
           inc=[(g_ev, 1)])
    p.step("gpsimd", lambda e: e.iota(coli, [[1, 128]], channel_multiplier=0,
                                      allow_small_or_imprecise_dtypes=True),
           inc=[(g_ev, 1)])
    p.step("vector", lambda e: e.tensor_tensor(ident_bf, ident_f, coli, ALU.is_equal),
           wait=[g_ev], inc=[(g_ev, 1)])
    p.step("vector", lambda e: e.tensor_tensor(ident_f, ident_f, coli, ALU.is_equal),
           wait=[g_ev], inc=[(g_ev, 1)])
    p.step("vector", lambda e: e.memset(ones_bf, 1.0), inc=[(g_ev, 1)])
    p.step("vector", lambda e: e.memset(ones64f, 1.0), inc=[(g_ev, 1)])
    p.step("vector", lambda e: e.memset(epsv, EPS), inc=[(g_ev, 1)])

    for nm, src, dstT in (("bq", BQ, bqT), ("bk", BK, bkT)):
        p.step("sync", lambda e, src=src: e.dma_start(
            out=bstg, in_=src.rearrange("(c p) -> c p", p=128)),
            wait=[(g_mm, g_mm.count)], inc=[(g_ld, 16)], tag=f"ldb_{nm}")
        p.step("tensor", lambda e: e.transpose(ps[:, 0:8], bstg, ident_f[0:8, 0:8]),
               wait=[g_ld, g_ev], inc=[(g_mm, 1)], tag=f"trb_{nm}")
        p.step("vector", lambda e, dstT=dstT: e.tensor_copy(dstT, ps[:, 0:8]),
               wait=[g_mm], inc=[(g_ev, 1)], tag=f"evb_{nm}")
    for nm, src, dst in (("bv", BV, bv_bf), ("bo", BO, bo_bf)):
        p.step("sync", lambda e, src=src: e.dma_start(
            out=brow, in_=src.rearrange("(a d) -> a d", a=1)),
            wait=[(g_ev, g_ev.count)], inc=[(g_ld, 16)], tag=f"ldr_{nm}")
        p.step("vector", lambda e, dst=dst: e.tensor_copy(dst, brow),
               wait=[g_ld], inc=[(g_ev, 1)], tag=f"cvr_{nm}")

    for nm, src in (("wq", WQ), ("wk", WK), ("wv", WV), ("wo", WO)):
        p.step("gpsimd", lambda e, src=src: e.dma_start(
            out=fstage[:, 0:8192], in_=src.rearrange("(c p) o -> p c o", p=128)),
            wait=[(g_st, g_st.count)], inc=[(g_ldg, 16)], tag=f"cast_{nm}")
        p.step("sync", lambda e, nm=nm: e.dma_start(
            out=wbf[nm].rearrange("(c p) o -> p c o", p=128), in_=fstage[:, 0:8192]),
            wait=[g_ldg], inc=[(g_st, 16)], tag=f"park_{nm}")
    for nm, src, dst in (("wg", WG, wg_bf), ("wu", WU, wu_bf), ("wd", WD, wd_bf)):
        p.step("gpsimd", lambda e, src=src: e.dma_start(
            out=fstage, in_=src.rearrange("(c p) o -> p c o", p=128)),
            wait=[(g_st, g_st.count)], inc=[(g_ldg, 16)], tag=f"cast_{nm}")
        p.step("sync", lambda e, dst=dst: e.dma_start(
            out=dst.rearrange("(c p) o -> p c o", p=128), in_=fstage),
            wait=[g_ldg], inc=[(g_st, 16)], tag=f"park_{nm}")

    for nm, srcs, dstT in (("wg", wg_bf, wgT_bf), ("wu", wu_bf, wuT_bf)):
        for c in range(8):
            for hg in range(6):
                p.step("scalar", lambda e, c=c, hg=hg, srcs=srcs: e.dma_start_transpose(
                    fstage2[:, c * 3072 + hg * 512: c * 3072 + (hg + 1) * 512],
                    srcs[hg * 512:(hg + 1) * 512, c * 128:(c + 1) * 128]),
                    wait=[(g_st, g_st.count)], inc=[(g_lds, 16)], tag=f"xp_{nm}{c}_{hg}")
        p.step("sync", lambda e, dstT=dstT: e.dma_start(
            out=dstT.rearrange("(c p) o -> p c o", p=128), in_=fstage2[:, 0:24576]),
            wait=[g_lds], inc=[(g_st, 16)], tag=f"parkT_{nm}")
    for hc in range(24):
        p.step("scalar", lambda e, hc=hc: e.dma_start_transpose(
            fstage2[:, hc * 1024:(hc + 1) * 1024], wd_bf[:, hc * 128:(hc + 1) * 128]),
            wait=[(g_st, g_st.count)], inc=[(g_lds, 16)], tag=f"xp_wd{hc}")
    p.step("sync", lambda e: e.dma_start(
        out=wdT_bf.rearrange("(c p) o -> p c o", p=128), in_=fstage2[:, 0:24576]),
        wait=[g_lds], inc=[(g_st, 16)], tag="parkT_wd")

    marks["prep"] = len(p.steps)
    prep_ev = (g_ev, g_ev.count)
    rel["arena"] = (g_st, g_st.count)
    rel["otm0"] = (g_ev, g_ev.count)
    rel["Wx0"] = (g_st, g_st.count)
    rel["Wx1"] = (g_st, g_st.count)
    rel["Wxo"] = (g_st, g_st.count)
    rel["banks03"] = (g_ev, g_ev.count)
    rel["bank6"] = prep_ev

    lxs = [p.sem("lxa"), p.sem("lxb"), p.sem("lxc")]
    vns = p.sem("vns")
    qz = p.sem("qz")
    lxi = [p.sem("lxia"), p.sem("lxib")]
    cs_x1 = [p.sem("csx1a"), p.sem("csx1b")]
    cs_l3 = [p.sem("csl3a"), p.sem("csl3b")]
    st2 = [p.sem("sta"), p.sem("stb")]
    lw = [p.sem("lwa"), p.sem("lwb")]
    lwo = p.sem("lwo")
    prn = p.sem("prn")
    ptr = p.sem("ptr")
    pev = p.sem("pev")
    kqm = p.sem("kqm")
    kqe = p.sem("kqe")
    vev = p.sem("vev")
    sxp = p.sem("sxp")
    epi = p.sem("epi")
    orw = p.sem("orw")
    oev = p.sem("oev")
    ce = p.sem("ce")
    fw = [p.sem("fwa"), p.sem("fwb")]
    fmm = p.sem("fmm")
    fge = p.sem("fge")
    fdm = p.sem("fdm")
    fda = p.sem("fda")
    fln = p.sem("fln")
    fep = p.sem("fep")
    st_sems = st2
    for s in range(NSEQ):
        R0 = s * S
        smm = p.sem(f"smm{s}")
        pvm = p.sem(f"pvm{s}")

        def ln_stats(csem, src_ap, waits=(), alt=False):
            bs, mv = (bstat2, mvt2) if alt else (bstat, mvt)
            p.step("vector", lambda e, src_ap=src_ap, bs=bs: e.bn_stats(
                bs[:, 0:6], src_ap[:, 0:512]),
                wait=list(waits) + [csem], inc=[(csem, 1)])
            p.step("vector", lambda e, src_ap=src_ap, bs=bs: e.bn_stats(
                bs[:, 6:12], src_ap[:, 512:1024]),
                inc=[(csem, 1)])
            p.step("vector", lambda e, bs=bs, mv=mv: e.bn_aggr(
                mv, bs.rearrange("p (g x) -> p g x", x=6)),
                wait=[csem], inc=[(csem, 1)])

        def ln_scalar_chain(csem, alt=False):
            mv, rstt, nm, lv = ((mvt2, rst2, nmr2, lvt2) if alt
                                else (mvt, rst, nmr, lvt))
            p.step("scalar", lambda e, mv=mv, lv=lv: e.activation(
                lv, mv[:, 1:2], AF.Ln, bias=epsv),
                wait=[csem], inc=[(csem, 1)])
            p.step("scalar", lambda e, rstt=rstt, lv=lv: e.activation(
                rstt, lv, AF.Exp, scale=-0.5),
                wait=[csem], inc=[(csem, 1)])
            p.step("vector", lambda e, mv=mv, rstt=rstt, nm=nm: e.scalar_tensor_tensor(
                nm, mv[:, 0:1], -1.0, rstt, ALU.mult, ALU.mult),
                wait=[csem], inc=[(csem, 1)])

        p.step("vector", lambda e: e.memset(Vaug, 1.0),
               wait=rw("arena", "arenab"), inc=[(vns, 1)], tag=f"s{s}vones")
        kqi = 0
        vi = 0
        for g in range(4):
            for t4 in range(4):
                t = 4 * g + t4
                sl = t % 3
                xs = xin[:, sl * 1024:(sl + 1) * 1024]
                lt = ln1_tm[:, (t % 2) * 1024:(t % 2 + 1) * 1024]
                p.step("sync", lambda e, t=t, xs=xs, R0=R0: e.dma_start(
                    out=xs, in_=X[R0 + 128 * t: R0 + 128 * (t + 1), :]),
                    wait=rw(f"xin{sl}"), inc=[(lxs[sl], 16)], tag=f"s{s}ldx{t}")
                ln_stats(prn, xs, waits=[lxs[sl]], alt=True)
                ln_scalar_chain(prn, alt=True)
                p.step("vector", lambda e, xs=xs, lt=lt: e.scalar_tensor_tensor(
                    lt, xs, rst2, nmr2.broadcast_to([128, 1024]), ALU.mult, ALU.add),
                    wait=[prn] + rw(f"ln3st{t % 2}"), inc=[(prn, 1)], tag=f"s{s}nrm{t}")
                rel[f"xin{sl}"] = (prn, prn.count)
                for c in range(8):
                    bcol = (6 + c % 2) * 1024 + (c // 2) * 128
                    slot = ps_bf[:, bcol:bcol + 128]
                    w = [(pev, max(0, pev.count - 1))]
                    if c == 0:
                        w.append((prn, prn.count))
                        if t == 0:
                            w += rw("bank6") + rw("bank7")
                            if s == 0:
                                w.append(prep_ev)
                    p.step("tensor", lambda e, lt=lt, c=c, slot=slot: e.transpose(
                        slot, lt[:, c * 128:(c + 1) * 128], ident_bf),
                        wait=w, inc=[(ptr, 1)], tag=f"s{s}tr{t}_{c}")
                    p.step("vector", lambda e, c=c, t4=t4, slot=slot: e.tensor_copy(
                        ln1_bl[:, c * 512 + t4 * 128: c * 512 + (t4 + 1) * 128], slot),
                        wait=[ptr] + (rw("ln1bl") if c == 0 and t4 == 0 else []),
                        inc=[(pev, 1)], tag=f"s{s}ev{t}_{c}")
                marks[f"A{s}t{t}"] = len(p.steps)
            marks[f"A{s}g{g}p"] = len(p.steps)
            ldidx = 0
            for wnm, dest in (("wk", "K"), ("wq", "Q"), ("wv", "V")):
                for half in range(2):
                    wsl = (g * 6 + ldidx) % 2
                    ldidx += 1
                    lws = lw[wsl]
                    p.step("gpsimd", lambda e, wnm=wnm, half=half, wsl=wsl: e.dma_start(
                        out=Wslot[wsl],
                        in_=wbf[wnm][:, 512 * half:512 * (half + 1)]
                        .rearrange("(c p) o -> p c o", p=128)),
                        wait=rw(f"Wx{wsl}"), inc=[(lws, 16)], tag=f"s{s}lw{g}{wnm}{half}")
                    if dest in ("K", "Q"):
                        out_fm = K_fm if dest == "K" else Q_fm
                        bT = bkT if dest == "K" else bqT
                        for do4 in range(4):
                            DO = 4 * half + do4
                            bk_i = 4 + (kqi % 2)
                            for di in range(8):
                                w = []
                                if di == 0:
                                    w = [(lws, lws.count), (pev, pev.count),
                                         (kqe, max(0, kqe.count - 1))]
                                    if kqi == 0:
                                        w += rw("banks45")
                                        if s == 0:
                                            w.append(prep_ev)
                                p.step("tensor", lambda e, wsl=wsl, di=di, do4=do4, bk_i=bk_i: e.matmul(
                                    bank(bk_i),
                                    Wslot[wsl][:, di * 512 + do4 * 128: di * 512 + (do4 + 1) * 128],
                                    ln1_bl[:, di * 512:(di + 1) * 512],
                                    start=(di == 0), stop=(di == 7)),
                                    wait=w, inc=([(kqm, 1)] if di == 7 else []),
                                    tag=f"s{s}b{dest}{g}{half}{do4}_{di}")
                            p.step("vector", lambda e, out_fm=out_fm, DO=DO, bT=bT, bk_i=bk_i, g=g: e.tensor_scalar(
                                out_fm[:, DO * 2048 + g * 512: DO * 2048 + (g + 1) * 512],
                                bank(bk_i), bT[:, DO:DO + 1], None, ALU.add),
                                wait=[kqm] + (rw("arena", "arenab") if kqi == 0 else []),
                                inc=[(kqe, 1)], tag=f"s{s}e{dest}{g}{half}{do4}")
                            kqi += 1
                    else:
                        for t4 in range(4):
                            kt = 4 * g + t4
                            bk_i = vi % 4
                            for di in range(8):
                                w = []
                                if di == 0:
                                    w = [(lws, lws.count), (pev, pev.count),
                                         (vev, max(0, vev.count - 3))]
                                    if vi == 0:
                                        w += rw("banks03")
                                p.step("tensor", lambda e, wsl=wsl, di=di, t4=t4, bk_i=bk_i: e.matmul(
                                    bank(bk_i),
                                    ln1_bl[:, di * 512 + t4 * 128: di * 512 + (t4 + 1) * 128],
                                    Wslot[wsl][:, di * 512:(di + 1) * 512],
                                    start=(di == 0), stop=False),
                                    wait=w, tag=f"s{s}bV{g}{half}{t4}_{di}")
                            p.step("tensor", lambda e, half=half, bk_i=bk_i: e.matmul(
                                bank(bk_i), ones_bf[0:1, :],
                                bv_bf[0:1, 512 * half:512 * (half + 1)],
                                start=False, stop=True),
                                inc=[(kqm, 1)], tag=f"s{s}bVb{g}{half}{t4}")
                            for hh in range(8):
                                hgl = 8 * half + hh
                                p.step("vector", lambda e, kt=kt, hgl=hgl, hh=hh, bk_i=bk_i: e.tensor_copy(
                                    Vaug[:, kt * 1040 + hgl * 65: kt * 1040 + hgl * 65 + 64],
                                    ps[:, 512 * bk_i + hh * 64: 512 * bk_i + (hh + 1) * 64]),
                                    wait=([kqm] if hh == 0 else []),
                                    inc=([(vev, 1)] if hh == 7 else []),
                                    tag=f"s{s}eV{g}{half}{t4}_{hh}")
                            vi += 1
                    rel[f"Wx{wsl}"] = (kqm, kqm.count)
            rel["ln1bl"] = (kqm, kqm.count)
            marks[f"A{s}g{g}"] = len(p.steps)
        p.step("gpsimd", lambda e: e.dma_start(
            out=Wo_s, in_=wbf["wo"].rearrange("(c p) o -> p c o", p=128)),
            wait=rw("Wxo"), inc=[(lwo, 16)], tag=f"s{s}lwo")

        marks[f"A{s}"] = len(p.steps)
        builds_done = [(kqe, kqe.count), (vev, vev.count), (kqm, kqm.count),
                       (vns, vns.count)]

        def plan_pv_pair(u, jp, sxp_val):
            h = u % 16
            pr = jp % 2
            for j in (2 * jp, 2 * jp + 1):
                wpv = [(sxp, sxp_val)]
                if j == 0:
                    wpv.append((epi, epi.count))
                p.step("tensor", lambda e, j=j, pr=pr, u=u, h=h: e.matmul(
                    ps[0:65, 2048 + 512 * (u % 2): 2048 + 512 * (u % 2) + 512],
                    Vaug[:, j * 1040 + h * 65: j * 1040 + h * 65 + 65],
                    Ebuf[:, 1024 * pr + 512 * (j % 2): 1024 * pr + 512 * (j % 2) + 512],
                    start=(j == 0), stop=(j == 15)),
                    wait=wpv, inc=([(pvm, 1)] if j % 2 == 1 else []),
                    tag=f"s{s}pv{u}_{j}")

        def plan_unit(u):
            g, h = u // 16, u % 16
            hc, hr = h // 2, 64 * (h % 2)
            hrc = 64 - hr
            z = u % 2
            wz = [(smm, max(0, smm.count - 8))]
            if u == 0:
                wz += builds_done
            p.step("vector", lambda e, z=z, hrc=hrc: e.memset(
                Q_z[hrc:hrc + 64, z * 512:(z + 1) * 512], 0.0),
                wait=wz, inc=[(qz, 1)], tag=f"s{s}qz0_{u}")
            p.step("vector", lambda e, z=z, hr=hr, hc=hc, g=g: e.tensor_copy(
                Q_z[hr:hr + 64, z * 512:(z + 1) * 512],
                Q_fm[hr:hr + 64, hc * 2048 + g * 512: hc * 2048 + (g + 1) * 512]),
                inc=[(qz, 1)], tag=f"s{s}qz1_{u}")
            for jp in range(8):
                P = u * 8 + jp
                pr = P % 2
                for j in (2 * jp, 2 * jp + 1):
                    w = []
                    if j % 2 == 0:
                        w.append((sxp, max(0, sxp.count - 1)))
                    if jp == 0 and j % 2 == 0:
                        w.append((qz, qz.count))
                    p.step("tensor", lambda e, j=j, pr=pr, hc=hc, z=z: e.matmul(
                        ps[:, 1024 * pr + 512 * (j % 2): 1024 * pr + 512 * (j % 2) + 512],
                        K_fm[:, hc * 2048 + j * 128: hc * 2048 + (j + 1) * 128],
                        Q_z[:, z * 512:(z + 1) * 512],
                        start=True, stop=True),
                        wait=w, inc=([(smm, 1)] if j % 2 == 1 else []),
                        tag=f"s{s}sT{u}_{j}")
                p.step("scalar", lambda e, pr=pr: e.activation(
                    Ebuf[:, 1024 * pr:1024 * (pr + 1)],
                    ps[:, 1024 * pr:1024 * (pr + 1)], AF.Exp, scale=0.125),
                    wait=[(smm, smm.count), (pvm, pvm.count)],
                    inc=[(sxp, 1)], tag=f"s{s}xp{u}_{P}")
                if PVLAG:
                    if jp >= 1:
                        plan_pv_pair(u, jp - 1, sxp.count - 1)
                else:
                    plan_pv_pair(u, jp, sxp.count)
            if PVLAG:
                plan_pv_pair(u, 7, sxp.count)

        def plan_epi1(u):
            p.step("vector", lambda e, u=u: e.tensor_copy(
                srow, ps[64:65, 2048 + 512 * (u % 2): 2048 + 512 * (u % 2) + 512]),
                wait=[(pvm, pvm.count), (epi, epi.count)],
                inc=[(epi, 1)], tag=f"s{s}e1_{u}")

        def plan_epi2(u):
            g, h = u // 16, u % 16
            hc, hr = h // 2, 64 * (h % 2)
            gb = (g % 2) * 4096
            p.step("scalar", lambda e: e.activation(lsr, srow, AF.Ln),
                   wait=[epi], inc=[(epi, 1)], tag=f"s{s}e2_{u}")
            p.step("scalar", lambda e: e.activation(rs, lsr, AF.Exp, scale=-1.0),
                   wait=[epi], inc=[(epi, 1)], tag=f"s{s}e3_{u}")
            if E4BF16:
                p.step("vector", lambda e: e.tensor_copy(rs_bf, rs),
                       wait=[epi], inc=[(epi, 1)], tag=f"s{s}e3b_{u}")
                p.step("tensor", lambda e: e.matmul(
                    ps[0:64, 3072:3584], ones_bf[0:1, 0:64], rs_bf[0:1, :],
                    start=True, stop=True),
                    wait=[epi], inc=[(epi, 1)], tag=f"s{s}e4_{u}")
            else:
                p.step("tensor", lambda e: e.matmul(
                    ps[0:64, 3072:3584], ones64f[0:1, :], rs[0:1, :],
                    start=True, stop=True),
                    wait=[epi], inc=[(epi, 1)], tag=f"s{s}e4_{u}")
            p.step("vector", lambda e: e.tensor_copy(rsb, ps[0:64, 3072:3584]),
                   wait=[epi], inc=[(epi, 1)], tag=f"s{s}e5_{u}")
            p.step("vector", lambda e, u=u, hr=hr, hc=hc, gb=gb: e.scalar_tensor_tensor(
                attn_fm[hr:hr + 64, gb + hc * 512: gb + (hc + 1) * 512],
                ps[0:64, 2048 + 512 * (u % 2): 2048 + 512 * (u % 2) + 512],
                1.0, rsb, ALU.mult, ALU.mult),
                wait=[epi, (orw, 8 * max(0, u // 16 - 1))],
                inc=[(epi, 1)], tag=f"s{s}e6_{u}")

        def plan_otile(g, t4, nlist=(0, 1)):
            gb = (g % 2) * 4096
            t = 4 * g + t4
            sl = t % 2
            for n in nlist:
                for di in range(8):
                    w = []
                    if di == 0:
                        w = [(epi, epi_group_done[g]), (lwo, lwo.count),
                             (oev, oev.count)]
                        if g == 0 and t4 == 0 and n == 0:
                            w += rw("bank7")
                    p.step("tensor", lambda e, di=di, t4=t4, n=n, gb=gb: e.matmul(
                        bank(7),
                        attn_fm[:, gb + di * 512 + t4 * 128: gb + di * 512 + (t4 + 1) * 128],
                        Wo_s[:, di * 1024 + n * 512: di * 1024 + (n + 1) * 512],
                        start=(di == 0), stop=False),
                        wait=w, tag=f"s{s}op{g}{t4}{n}_{di}")
                p.step("tensor", lambda e, n=n: e.matmul(
                    bank(7), ones_bf[0:1, :], bo_bf[0:1, n * 512:(n + 1) * 512],
                    start=False, stop=True),
                    inc=[(orw, 1)], tag=f"s{s}opb{g}{t4}{n}")
                p.step("scalar", lambda e, sl=sl, n=n: e.activation(
                    o_tm[:, sl * 1024 + n * 512: sl * 1024 + (n + 1) * 512],
                    bank(7), AF.Copy),
                    wait=[(orw, orw.count)] + (rw(f"otm{sl}") if n == 0 else []),
                    inc=[(oev, 1)], tag=f"s{s}oe{g}{t4}{n}")

        def plan_ctile(g, t4):
            t = 4 * g + t4
            sl = t % 2
            xsl = t % 3
            ot = o_tm[:, sl * 1024:(sl + 1) * 1024]
            xs = xin[:, xsl * 1024:(xsl + 1) * 1024]
            xv = x1st[:, sl * 1024:(sl + 1) * 1024]
            lv = ln3st[:, sl * 1024:(sl + 1) * 1024]
            p.step("sync", lambda e, t=t, xs=xs, R0=R0: e.dma_start(
                out=xs, in_=X[R0 + 128 * t: R0 + 128 * (t + 1), :]),
                wait=rw(f"xin{xsl}"), inc=[(lxs[xsl], 16)], tag=f"s{s}rx{t}")
            ln_stats(ce, ot, waits=[(oev, oev.count)] + rw("ln_scalars"))
            ln_scalar_chain(ce)
            p.step("vector", lambda e, ot=ot: e.scalar_tensor_tensor(
                o_n, ot, rst, nmr.broadcast_to([128, 1024]), ALU.mult, ALU.add),
                wait=[ce] + rw("o_n"), inc=[(ce, 1)], tag=f"s{s}on{t}")
            rel[f"otm{sl}"] = (ce, ce.count)
            rel["ln_scalars"] = (ce, ce.count)
            p.step("vector", lambda e, xv=xv, xs=xs: e.tensor_add(xv, o_n, xs),
                   wait=[ce, (lxs[xsl], lxs[xsl].count)] + rw(f"x1st{sl}"),
                   inc=[(ce, 1)], tag=f"s{s}x1{t}")
            rel["o_n"] = (ce, ce.count)
            rel[f"xin{xsl}"] = (ce, ce.count)
            ln_stats(ce, xv)
            ln_scalar_chain(ce)
            p.step("vector", lambda e, xv=xv, lv=lv: e.scalar_tensor_tensor(
                lv, xv, rst, nmr.broadcast_to([128, 1024]), ALU.mult, ALU.add),
                wait=[ce] + rw(f"ln3st{sl}"), inc=[(ce, 1)], tag=f"s{s}l3{t}")
            rel["ln_scalars"] = (ce, ce.count)
            p.step("sync", lambda e, t=t, xv=xv, R0=R0: e.dma_start(
                out=x1_s[R0 + 128 * t: R0 + 128 * (t + 1), :], in_=xv),
                wait=[(ce, ce.count)], inc=[(cs_x1[sl], 16)], tag=f"s{s}sx1{t}")
            rel[f"x1st{sl}"] = (cs_x1[sl], cs_x1[sl].count)
            p.step("sync", lambda e, t=t, lv=lv, R0=R0: e.dma_start(
                out=ln3_s[R0 + 128 * t: R0 + 128 * (t + 1), :], in_=lv),
                inc=[(cs_l3[sl], 16)], tag=f"s{s}sl3{t}")
            rel[f"ln3st{sl}"] = (cs_l3[sl], cs_l3[sl].count)

        ln3_ready = {}
        epi_group_done = {}
        for u in range(65):
            if u < 64:
                plan_unit(u)
            if u >= 1:
                plan_epi2(u - 1)
                if (u - 1) % 16 == 15:
                    epi_group_done[(u - 1) // 16] = epi.count
            if u < 64:
                plan_epi1(u)
            gp = u // 16 - 1
            idx = u % 16
            if SPREAD_OPROJ:
                if gp >= 0 and 1 <= idx <= 8:
                    t4, n = (idx - 1) // 2, (idx - 1) % 2
                    plan_otile(gp, t4, nlist=(n,))
                    if n == 1:
                        plan_ctile(gp, t4)
                        if t4 == 3:
                            ln3_ready[gp] = [(cs_l3[0], cs_l3[0].count),
                                             (cs_l3[1], cs_l3[1].count)]
                if u == 64:
                    for t4 in range(4):
                        plan_otile(3, t4)
                        plan_ctile(3, t4)
                    ln3_ready[3] = [(cs_l3[0], cs_l3[0].count),
                                    (cs_l3[1], cs_l3[1].count)]
            else:
                if u >= 1 and (u - 1) % 16 == 15:
                    g = (u - 1) // 16
                    for t4 in range(4):
                        plan_otile(g, t4)
                        plan_ctile(g, t4)
                    ln3_ready[g] = [(cs_l3[0], cs_l3[0].count),
                                    (cs_l3[1], cs_l3[1].count)]
        marks[f"B{s}"] = len(p.steps)
        attn_done = [(sxp, sxp.count), (epi, epi.count), (oev, oev.count),
                     (orw, orw.count), (smm, smm.count)]

        rel["fw0"] = (smm, smm.count)
        rel["fw1"] = (smm, smm.count)
        rel["ln3bl"] = (smm, smm.count)
        rel["accH"] = (epi, epi.count)

        hsg = 0
        ggi = 0
        for Hh in range(2):
            for u2 in range(2):
                for di in range(8):
                    w = list(ln3_ready[2 * Hh + u2])
                    if u2 == 0 and di == 0:
                        w += rw("ln3bl")
                        w.append((fmm, fmm.count))
                    p.step("sync", lambda e, Hh=Hh, u2=u2, di=di, R0=R0: e.dma_start_transpose(
                        ln3_bl[:, di * 1024 + u2 * 512: di * 1024 + (u2 + 1) * 512],
                        ln3_s[R0 + 1024 * Hh + 512 * u2: R0 + 1024 * Hh + 512 * (u2 + 1),
                              di * 128:(di + 1) * 128]),
                        wait=w, inc=[(fln, 16)], tag=f"s{s}fl{Hh}{u2}_{di}")
            rel["ln1bl"] = (fln, fln.count)
            fln_done = fln.count
            fge_done = {}
            down_pending = []

            def plan_down(job):
                hs, hsl, fsl = job
                for tt in range(8):
                    dbk = 4 + 2 * (tt % 2)
                    at = acc[:, 1024 * tt:1024 * (tt + 1)]
                    for c in range(2):
                        for n in range(2):
                            w = []
                            if c == 0 and n == 0:
                                w = [(fge, fge_done[hs]),
                                     (fda, max(0, fda.count - 1))]
                                if hs == 0 and tt < 2:
                                    w.append((fln, fln_done))
                                    if Hh == 0:
                                        w += attn_done
                            p.step("tensor", lambda e, c=c, n=n, tt=tt, hsl=hsl, fsl=fsl, dbk=dbk: e.matmul(
                                bank(dbk + n),
                                h_sfm[hsl][:, c * 1024 + tt * 128: c * 1024 + (tt + 1) * 128],
                                wdT_s[fsl][:, c * 1024 + n * 512: c * 1024 + (n + 1) * 512],
                                start=(c == 0), stop=(c == 1)),
                                wait=w,
                                inc=([(fdm, 1)] if c == 1 and n == 1 else []),
                                tag=f"s{s}d{Hh}{hs}{tt}_{c}{n}")
                    if hs == 0:
                        p.step("vector", lambda e, at=at, dbk=dbk: e.tensor_scalar(
                            at, bank2(dbk), QSC, None, ALU.mult),
                            wait=[(fdm, fdm.count)] + (rw("accH") if tt == 0 and Hh == 0 else []),
                            inc=[(fda, 1)], tag=f"s{s}da{Hh}{hs}{tt}")
                    else:
                        p.step("vector", lambda e, at=at, dbk=dbk: e.scalar_tensor_tensor(
                            at, bank2(dbk), QSC, at, ALU.mult, ALU.add),
                            wait=[(fdm, fdm.count)],
                            inc=[(fda, 1)], tag=f"s{s}da{Hh}{hs}{tt}")
                rel[f"fw{fsl}"] = (fdm, fdm.count)

            fln_done = fln.count
            for hs in range(12):
                fsl = hsg % 2
                hof = hs * 256
                fws = fw[fsl]
                for wsrc, wdst in ((wgT_bf, wgT_s), (wuT_bf, wuT_s)):
                    p.step("gpsimd", lambda e, wsrc=wsrc, wdst=wdst, fsl=fsl, hof=hof: e.dma_start(
                        out=wdst[fsl],
                        in_=wsrc[:, hof:hof + 256].rearrange("(c p) h -> p c h", p=128)),
                        wait=rw(f"fw{fsl}"), inc=[(fws, 16)], tag=f"s{s}fw{Hh}{hs}")
                p.step("gpsimd", lambda e, fsl=fsl, hof=hof: e.dma_start(
                    out=wdT_s[fsl],
                    in_=wdT_bf[hof:hof + 256, :].rearrange("(c p) d -> p c d", p=128)),
                    inc=[(fws, 16)], tag=f"s{s}fwd{Hh}{hs}")
                hsl = hsg % 2
                for u2 in range(2):
                    for c in range(2):
                        gbk = ggi % 2
                        ubk = 2 + ggi % 2
                        gsl = ggi % 2
                        for di in range(8):
                            w = []
                            if di == 0:
                                w = [(fws, fws.count), (fln, fln.count),
                                     (fge, max(0, fge.count - 2))]
                                if Hh == 0 and hs == 0 and u2 == 0 and c == 0:
                                    w += attn_done
                            p.step("tensor", lambda e, di=di, c=c, u2=u2, fsl=fsl, gbk=gbk: e.matmul(
                                bank(gbk),
                                wgT_s[fsl][:, di * 256 + c * 128: di * 256 + (c + 1) * 128],
                                ln3_bl[:, di * 1024 + u2 * 512: di * 1024 + (u2 + 1) * 512],
                                start=(di == 0), stop=(di == 7)),
                                wait=w, tag=f"s{s}g{Hh}{hs}{u2}{c}_{di}")
                            p.step("tensor", lambda e, di=di, c=c, u2=u2, fsl=fsl, ubk=ubk: e.matmul(
                                bank(ubk),
                                wuT_s[fsl][:, di * 256 + c * 128: di * 256 + (c + 1) * 128],
                                ln3_bl[:, di * 1024 + u2 * 512: di * 1024 + (u2 + 1) * 512],
                                start=(di == 0), stop=(di == 7)),
                                inc=([(fmm, 1)] if di == 7 else []),
                                tag=f"s{s}uu{Hh}{hs}{u2}{c}_{di}")
                        p.step("scalar", lambda e, gbk=gbk, gsl=gsl: e.activation(
                            gbf[:, gsl * 512:(gsl + 1) * 512], bank(gbk), AF.Silu,
                            scale=QSC),
                            wait=[(fmm, fmm.count), (fge, fge.count)],
                            inc=[(fge, 1)], tag=f"s{s}sl{Hh}{hs}{u2}{c}")
                        p.step("vector", lambda e, ubk=ubk, gsl=gsl, hsl=hsl, c=c, u2=u2: e.scalar_tensor_tensor(
                            h_sfm[hsl][:, c * 1024 + u2 * 512: c * 1024 + (u2 + 1) * 512],
                            bank(ubk), QSC, gbf[:, gsl * 512:(gsl + 1) * 512],
                            ALU.mult, ALU.mult),
                            wait=[(fge, fge.count), (fdm, fdm.count)],
                            inc=[(fge, 1)], tag=f"s{s}h{Hh}{hs}{u2}{c}")
                        ggi += 1
                fge_done[hs] = fge.count
                down_pending.append((hs, hsl, fsl))
                if hs >= 1:
                    plan_down(down_pending.pop(0))
                if hs == 11:
                    plan_down(down_pending.pop(0))
                hsg += 1
            for tt in range(8):
                t = 8 * Hh + tt
                sl = t % 2
                at = acc[:, 1024 * tt:1024 * (tt + 1)]
                ov = o4[:, sl * 1024:(sl + 1) * 1024]
                xs = x1st[:, sl * 1024:(sl + 1) * 1024]
                p.step("sync", lambda e, t=t, xs=xs, R0=R0: e.dma_start(
                    out=xs, in_=x1_s[R0 + 128 * t: R0 + 128 * (t + 1), :]),
                    wait=rw(f"x1st{sl}") + [(cs_x1[sl], cs_x1[sl].count)],
                    inc=[(lxi[sl], 16)], tag=f"s{s}fx1{t}")
                ln_stats(fep, at, waits=[(fda, fda.count)] + rw("ln_scalars"))
                ln_scalar_chain(fep)
                p.step("vector", lambda e, at=at, ov=ov: e.scalar_tensor_tensor(
                    ov, at, rst, nmr.broadcast_to([128, 1024]), ALU.mult, ALU.add),
                    wait=[fep] + rw(f"o4{sl}"), inc=[(fep, 1)], tag=f"s{s}f4{t}")
                rel["ln_scalars"] = (fep, fep.count)
                p.step("vector", lambda e, ov=ov, xs=xs: e.tensor_add(ov, ov, xs),
                       wait=[fep, (lxi[sl], lxi[sl].count)], inc=[(fep, 1)],
                       tag=f"s{s}f5{t}")
                rel[f"x1st{sl}"] = (fep, fep.count)
                p.step("sync", lambda e, t=t, ov=ov, R0=R0: e.dma_start(
                    out=OUT[R0 + 128 * t: R0 + 128 * (t + 1), :], in_=ov),
                    wait=[(fep, fep.count)], inc=[(st2[sl], 16)], tag=f"s{s}out{t}")
                rel[f"o4{sl}"] = (st2[sl], st2[sl].count)
            rel["accH"] = (fep, fep.count)

        marks[f"D{s}"] = len(p.steps)
        rel["arena"] = (st2[1], st2[1].count)
        rel["arenab"] = (st2[0], st2[0].count)
        rel["bank6"] = (fda, fda.count)
        rel["bank7"] = (fda, fda.count)
        rel["banks03"] = (fge, fge.count)
        rel["banks45"] = (fda, fda.count)
        rel["Wxo"] = (orw, orw.count)

    if truncate_at is not None:
        p.steps = p.steps[:truncate_at]
    elif truncate is not None:
        p.steps = p.steps[:marks[truncate]]
    else:
        p.step("sync", lambda e: e.nop(),
               wait=[(stx, stx.count) for stx in st_sems], tag="final")

    build_program.marks = dict(marks)
    build_program.plan = p
    p.emit(nc, es)
    es.close()
    return nc


_NC_CACHE = None


def kernel(L0_attn_x, L0_attn_Wq, L0_attn_bq, L0_attn_Wk, L0_attn_bk,
           L0_attn_Wv, L0_attn_bv, L0_attn_Wo, L0_attn_bo,
           L0_ffn_w_gate, L0_ffn_w_up, L0_ffn_w_down):
    global _NC_CACHE
    if _NC_CACHE is None:
        _NC_CACHE = build_program()
    nc = _NC_CACHE

    x = np.ascontiguousarray(np.asarray(L0_attn_x, dtype=np.float32))
    base = dict(
        wq=np.ascontiguousarray(np.asarray(L0_attn_Wq, dtype=np.float32)),
        wk=np.ascontiguousarray(np.asarray(L0_attn_Wk, dtype=np.float32)),
        wv=np.ascontiguousarray(np.asarray(L0_attn_Wv, dtype=np.float32)),
        wo=np.ascontiguousarray(np.asarray(L0_attn_Wo, dtype=np.float32)),
        bq=np.ascontiguousarray(np.asarray(L0_attn_bq, dtype=np.float32)),
        bk=np.ascontiguousarray(np.asarray(L0_attn_bk, dtype=np.float32)),
        bv=np.ascontiguousarray(np.asarray(L0_attn_bv, dtype=np.float32)),
        bo=np.ascontiguousarray(np.asarray(L0_attn_bo, dtype=np.float32)),
        wgi=np.ascontiguousarray(np.asarray(L0_ffn_w_gate, dtype=np.int32)),
        wui=np.ascontiguousarray(np.asarray(L0_ffn_w_up, dtype=np.int32)),
        wdi=np.ascontiguousarray(np.asarray(L0_ffn_w_down, dtype=np.int32)),
    )
    in_maps = []
    for i in range(8):
        m = dict(base)
        m["x"] = np.ascontiguousarray(
            x[NSEQ * i:NSEQ * (i + 1)].reshape(NSEQ * S, D))
        in_maps.append(m)

    res = run_bass_kernel_spmd(nc, in_maps, core_ids=list(range(8)))
    outs = [np.asarray(r["out"]).reshape(NSEQ, S, D) for r in res.results]
    return np.concatenate(outs, axis=0).astype(np.float32)


# version 08 (final)
# speedup vs baseline: 1.0395x; 1.0193x over previous
import contextlib
import numpy as np
import concourse.bass as bass
import concourse.mybir as mybir
from concourse.bass_utils import run_bass_kernel_spmd

F32 = mybir.dt.float32
BF16 = mybir.dt.bfloat16
I32 = mybir.dt.int32
AF = mybir.ActivationFunctionType
ALU = mybir.AluOpType

D = 1024
S = 2048
NSEQ = 4
FF = 3072
EPS = 1e-5
QSC = 1.0 / 127.0
SPREAD_OPROJ = True
PVLAG = True
E4BF16 = False


class PSem:
    __slots__ = ("name", "count", "handle")

    def __init__(self, name):
        self.name = name
        self.count = 0
        self.handle = None


class Plan:
    def __init__(self):
        self.steps = []
        self.sems = []

    def sem(self, name):
        s = PSem(name)
        self.sems.append(s)
        return s

    def step(self, engine, fn, wait=(), inc=(), tag=""):
        waits = []
        for w in wait:
            if w is None:
                continue
            if isinstance(w, PSem):
                waits.append((w, w.count))
            else:
                waits.append(w)
        incs = list(inc)
        for sm, amt in incs:
            sm.count += amt
        self.steps.append((engine, fn, waits, incs, tag))

    def emit(self, nc, es):
        for sm in self.sems:
            sm.handle = es.enter_context(nc.semaphore(sm.name))
        block = es.enter_context(nc.Block())
        steps = self.steps

        self.emitted = [None] * len(steps)

        def mk(engine_name):
            def body(eng):
                waited = {}
                for si, (engine, fn, waits, incs, tag) in enumerate(steps):
                    if engine != engine_name:
                        continue
                    for sm, val in waits:
                        if val > 0 and waited.get(id(sm), -1) < val:
                            eng.wait_ge(sm.handle, val)
                            waited[id(sm)] = val
                    inst = fn(eng)
                    self.emitted[si] = inst
                    if incs:
                        assert inst is not None, f"step {tag} needs inst for incs"
                        for sm, amt in incs:
                            inst = inst.then_inc(sm.handle, amt)
            return body

        used = {st[0] for st in steps}
        for name, opener in (("sync", block.sync), ("tensor", block.tensor),
                             ("vector", block.vector), ("scalar", block.scalar),
                             ("gpsimd", block.gpsimd)):
            if name in used:
                opener(mk(name))


def build_program(truncate=None, truncate_at=None, debug_scratch=False):
    nc = bass.Bass("TRN2", target_bir_lowering=False)

    X = nc.dram_tensor("x", [NSEQ * S, D], F32, kind="ExternalInput").ap()
    OUT = nc.dram_tensor("out", [NSEQ * S, D], F32, kind="ExternalOutput").ap()
    WQ = nc.dram_tensor("wq", [D, D], F32, kind="ExternalInput").ap()
    WK = nc.dram_tensor("wk", [D, D], F32, kind="ExternalInput").ap()
    WV = nc.dram_tensor("wv", [D, D], F32, kind="ExternalInput").ap()
    WO = nc.dram_tensor("wo", [D, D], F32, kind="ExternalInput").ap()
    BQ = nc.dram_tensor("bq", [D], F32, kind="ExternalInput").ap()
    BK = nc.dram_tensor("bk", [D], F32, kind="ExternalInput").ap()
    BV = nc.dram_tensor("bv", [D], F32, kind="ExternalInput").ap()
    BO = nc.dram_tensor("bo", [D], F32, kind="ExternalInput").ap()
    WG = nc.dram_tensor("wgi", [FF, D], I32, kind="ExternalInput").ap()
    WU = nc.dram_tensor("wui", [FF, D], I32, kind="ExternalInput").ap()
    WD = nc.dram_tensor("wdi", [D, FF], I32, kind="ExternalInput").ap()

    wbf = {}
    for nm in ("wq", "wk", "wv", "wo"):
        wbf[nm] = nc.dram_tensor(nm + "_bf", [D, D], BF16, kind="Internal").ap()
    wg_bf = nc.dram_tensor("wg_bf", [FF, D], BF16, kind="Internal").ap()
    wu_bf = nc.dram_tensor("wu_bf", [FF, D], BF16, kind="Internal").ap()
    wd_bf = nc.dram_tensor("wd_bf", [D, FF], BF16, kind="Internal").ap()
    wgT_bf = nc.dram_tensor("wgT_bf", [D, FF], BF16, kind="Internal").ap()
    wuT_bf = nc.dram_tensor("wuT_bf", [D, FF], BF16, kind="Internal").ap()
    wdT_bf = nc.dram_tensor("wdT_bf", [FF, D], BF16, kind="Internal").ap()
    skind = "ExternalOutput" if debug_scratch else "Internal"
    x1_s = nc.dram_tensor("x1_s", [NSEQ * S, D], F32, kind=skind).ap()
    ln3_s = nc.dram_tensor("ln3_s", [NSEQ * S, D], BF16, kind=skind).ap()

    es = contextlib.ExitStack()
    sb = lambda n, shp, d: es.enter_context(nc.sbuf_tensor(n, shp, d)).ap()

    ident_bf = sb("ident_bf", [128, 128], BF16)
    ones_bf = sb("ones_bf", [1, 128], BF16)
    ones64f = sb("ones64f", [1, 64], F32)
    bqT = sb("bqT", [128, 8], F32)
    bkT = sb("bkT", [128, 8], F32)

    bv_bf = sb("bv_bf", [1, D], BF16)
    bo_bf = sb("bo_bf", [1, D], BF16)
    brow = sb("brow", [1, D], F32)

    AR = sb("arena", [128, 49408], BF16)
    K_fm = AR[:, 0:16384]
    Q_fm = AR[:, 16384:32768]
    Vaug = AR[:, 32768:49408]
    fstage = AR[:, 0:24576]
    fstage2 = AR[:, 24576:49152]
    ln3_bl = AR[:, 0:8192]
    h_sfm = [AR[:, 8192:10240], AR[:, 10240:12288]]
    wgT_s = [AR[:, 12288:14336], AR[:, 14336:16384]]
    wuT_s = [AR[:, 16384:18432], AR[:, 18432:20480]]
    wdT_s = [AR[:, 20480:22528], AR[:, 22528:24576]]
    acc = AR[:, 24576:40960].bitcast(F32)
    o4 = AR[:, 40960:45056].bitcast(F32)

    ln1_bl = sb("ln1_bl", [128, 4096], BF16)
    Ebuf = sb("Ebuf", [128, 2048], BF16)
    attn_fm = sb("attn_fm", [128, 8192], BF16)
    srow = brow[0:1, 0:512]
    lsr = brow[0:1, 512:1024]
    rs = sb("rs", [1, 512], F32)
    rsb = sb("rsb", [64, 512], F32)
    o_tm = sb("o_tm", [128, 2048], F32)
    ident_f = o_tm[:, 256:384]
    STS = [o_tm[:, 384:396], o_tm[:, 532:544]]
    bstg = o_tm[0:8, 401:529]
    o_n = sb("o_n", [128, 1024], F32)
    xin = sb("xin", [128, 3072], F32)
    x1st = sb("x1st", [128, 2048], F32)
    ln3st = sb("ln3st", [128, 2048], BF16)
    ln1_tm = ln3st
    Wx = sb("Wx", [128, 16384], BF16)
    Wslot = [Wx[:, 0:4096], Wx[:, 4096:8192]]
    Wo_s = Wx[:, 8192:16384]
    gbf = sb("gbf", [128, 1024], BF16)
    rs_bf = gbf[0:1, 0:512]
    Q_z = gbf
    bstat = sb("bstat", [128, 12], F32)
    mvt = sb("mvt", [128, 2], F32)
    rst = sb("rst", [128, 1], F32)
    nmr = sb("nmr", [128, 1], F32)
    lvt = sb("lvt", [128, 1], F32)
    epsv = sb("epsv", [128, 1], F32)

    ps = es.enter_context(nc.psum_tensor("ps", [128, 4096], F32)).ap()
    ps_bf = ps.bitcast(BF16)

    def bank(b):
        return ps[:, 512 * b:512 * (b + 1)]

    def bank2(b):
        return ps[:, 512 * b:512 * (b + 2)]

    p = Plan()
    marks = {}
    rel = {}

    def rw(*names):
        return [rel[n] for n in names if n in rel]

    g_ld = p.sem("g_ld")
    g_ldg = p.sem("g_ldg")
    g_lds = p.sem("g_lds")
    g_st = p.sem("g_st")
    g_ev = p.sem("g_ev")
    g_mm = p.sem("g_mm")

    coli = o_tm[:, 0:128]
    p.step("gpsimd", lambda e: e.iota(ident_f, [[0, 128]], channel_multiplier=1,
                                      allow_small_or_imprecise_dtypes=True),
           inc=[(g_ev, 1)])
    p.step("gpsimd", lambda e: e.iota(coli, [[1, 128]], channel_multiplier=0,
                                      allow_small_or_imprecise_dtypes=True),
           inc=[(g_ev, 1)])
    p.step("vector", lambda e: e.tensor_tensor(ident_bf, ident_f, coli, ALU.is_equal),
           wait=[g_ev], inc=[(g_ev, 1)])
    p.step("vector", lambda e: e.tensor_tensor(ident_f, ident_f, coli, ALU.is_equal),
           wait=[g_ev], inc=[(g_ev, 1)])
    p.step("vector", lambda e: e.memset(ones_bf, 1.0), inc=[(g_ev, 1)])
    p.step("vector", lambda e: e.memset(ones64f, 1.0), inc=[(g_ev, 1)])
    p.step("vector", lambda e: e.memset(epsv, EPS), inc=[(g_ev, 1)])

    for nm, src, dstT in (("bq", BQ, bqT), ("bk", BK, bkT)):
        p.step("sync", lambda e, src=src: e.dma_start(
            out=bstg, in_=src.rearrange("(c p) -> c p", p=128)),
            wait=[(g_mm, g_mm.count)], inc=[(g_ld, 16)], tag=f"ldb_{nm}")
        p.step("tensor", lambda e: e.transpose(ps[:, 0:8], bstg, ident_f[0:8, 0:8]),
               wait=[g_ld, g_ev], inc=[(g_mm, 1)], tag=f"trb_{nm}")
        p.step("vector", lambda e, dstT=dstT: e.tensor_copy(dstT, ps[:, 0:8]),
               wait=[g_mm], inc=[(g_ev, 1)], tag=f"evb_{nm}")
    for nm, src, dst in (("bv", BV, bv_bf), ("bo", BO, bo_bf)):
        p.step("sync", lambda e, src=src: e.dma_start(
            out=brow, in_=src.rearrange("(a d) -> a d", a=1)),
            wait=[(g_ev, g_ev.count)], inc=[(g_ld, 16)], tag=f"ldr_{nm}")
        p.step("vector", lambda e, dst=dst: e.tensor_copy(dst, brow),
               wait=[g_ld], inc=[(g_ev, 1)], tag=f"cvr_{nm}")

    for nm, src in (("wq", WQ), ("wk", WK), ("wv", WV), ("wo", WO)):
        p.step("gpsimd", lambda e, src=src: e.dma_start(
            out=fstage[:, 0:8192], in_=src.rearrange("(c p) o -> p c o", p=128)),
            wait=[(g_st, g_st.count)], inc=[(g_ldg, 16)], tag=f"cast_{nm}")
        p.step("sync", lambda e, nm=nm: e.dma_start(
            out=wbf[nm].rearrange("(c p) o -> p c o", p=128), in_=fstage[:, 0:8192]),
            wait=[g_ldg], inc=[(g_st, 16)], tag=f"park_{nm}")
    for nm, src, dst in (("wg", WG, wg_bf), ("wu", WU, wu_bf), ("wd", WD, wd_bf)):
        p.step("gpsimd", lambda e, src=src: e.dma_start(
            out=fstage, in_=src.rearrange("(c p) o -> p c o", p=128)),
            wait=[(g_st, g_st.count)], inc=[(g_ldg, 16)], tag=f"cast_{nm}")
        p.step("sync", lambda e, dst=dst: e.dma_start(
            out=dst.rearrange("(c p) o -> p c o", p=128), in_=fstage),
            wait=[g_ldg], inc=[(g_st, 16)], tag=f"park_{nm}")

    for nm, srcs, dstT in (("wg", wg_bf, wgT_bf), ("wu", wu_bf, wuT_bf)):
        for c in range(8):
            for hg in range(6):
                p.step("scalar", lambda e, c=c, hg=hg, srcs=srcs: e.dma_start_transpose(
                    fstage2[:, c * 3072 + hg * 512: c * 3072 + (hg + 1) * 512],
                    srcs[hg * 512:(hg + 1) * 512, c * 128:(c + 1) * 128]),
                    wait=[(g_st, g_st.count)], inc=[(g_lds, 16)], tag=f"xp_{nm}{c}_{hg}")
        p.step("sync", lambda e, dstT=dstT: e.dma_start(
            out=dstT.rearrange("(c p) o -> p c o", p=128), in_=fstage2[:, 0:24576]),
            wait=[g_lds], inc=[(g_st, 16)], tag=f"parkT_{nm}")
    for hc in range(24):
        p.step("scalar", lambda e, hc=hc: e.dma_start_transpose(
            fstage2[:, hc * 1024:(hc + 1) * 1024], wd_bf[:, hc * 128:(hc + 1) * 128]),
            wait=[(g_st, g_st.count)], inc=[(g_lds, 16)], tag=f"xp_wd{hc}")
    p.step("sync", lambda e: e.dma_start(
        out=wdT_bf.rearrange("(c p) o -> p c o", p=128), in_=fstage2[:, 0:24576]),
        wait=[g_lds], inc=[(g_st, 16)], tag="parkT_wd")

    marks["prep"] = len(p.steps)
    prep_ev = (g_ev, g_ev.count)
    rel["arena"] = (g_st, g_st.count)
    rel["otm0"] = (g_ev, g_ev.count)
    rel["Wx0"] = (g_st, g_st.count)
    rel["Wx1"] = (g_st, g_st.count)
    rel["Wxo"] = (g_st, g_st.count)
    rel["banks03"] = (g_ev, g_ev.count)
    rel["bank6"] = prep_ev

    lxs = [p.sem("lxa"), p.sem("lxb"), p.sem("lxc")]
    vns = p.sem("vns")
    qz = p.sem("qz")
    lxi = [p.sem("lxia"), p.sem("lxib")]
    cs_x1 = [p.sem("csx1a"), p.sem("csx1b")]
    cs_l3 = [p.sem("csl3a"), p.sem("csl3b")]
    st2 = [p.sem("sta"), p.sem("stb")]
    lw = [p.sem("lwa"), p.sem("lwb")]
    lwo = p.sem("lwo")
    prnA = p.sem("prnA")
    prnD = p.sem("prnD")
    ptr = p.sem("ptr")
    pev = p.sem("pev")
    kqm = p.sem("kqm")
    kqe = p.sem("kqe")
    vev = p.sem("vev")
    sxp = p.sem("sxp")
    epi = p.sem("epi")
    orw = p.sem("orw")
    oev = p.sem("oev")
    ce = p.sem("ce")
    fw = [p.sem("fwa"), p.sem("fwb")]
    fmm = p.sem("fmm")
    fge = p.sem("fge")
    fdm = p.sem("fdm")
    fda = p.sem("fda")
    fln = p.sem("fln")
    fep = p.sem("fep")
    st_sems = st2
    for s in range(NSEQ):
        R0 = s * S
        smm = p.sem(f"smm{s}")
        pvm = p.sem(f"pvm{s}")

        def ln_stats(csem, src_ap, waits=(), alt=False):
            bs, mv = (bstat2, mvt2) if alt else (bstat, mvt)
            p.step("vector", lambda e, src_ap=src_ap, bs=bs: e.bn_stats(
                bs[:, 0:6], src_ap[:, 0:512]),
                wait=list(waits) + [csem], inc=[(csem, 1)])
            p.step("vector", lambda e, src_ap=src_ap, bs=bs: e.bn_stats(
                bs[:, 6:12], src_ap[:, 512:1024]),
                inc=[(csem, 1)])
            p.step("vector", lambda e, bs=bs, mv=mv: e.bn_aggr(
                mv, bs.rearrange("p (g x) -> p g x", x=6)),
                wait=[csem], inc=[(csem, 1)])

        def ln_scalar_chain(csem, alt=False):
            mv, rstt, nm, lv = ((mvt2, rst2, nmr2, lvt2) if alt
                                else (mvt, rst, nmr, lvt))
            p.step("scalar", lambda e, mv=mv, lv=lv: e.activation(
                lv, mv[:, 1:2], AF.Ln, bias=epsv),
                wait=[csem], inc=[(csem, 1)])
            p.step("scalar", lambda e, rstt=rstt, lv=lv: e.activation(
                rstt, lv, AF.Exp, scale=-0.5),
                wait=[csem], inc=[(csem, 1)])
            p.step("vector", lambda e, mv=mv, rstt=rstt, nm=nm: e.scalar_tensor_tensor(
                nm, mv[:, 0:1], -1.0, rstt, ALU.mult, ALU.mult),
                wait=[csem], inc=[(csem, 1)])

        p.step("vector", lambda e: e.memset(Vaug, 1.0),
               wait=rw("arena", "arenab"), inc=[(vns, 1)], tag=f"s{s}vones")
        kqi = 0
        vi = 0
        for g in range(4):
            for t4 in range(4):
                t = 4 * g + t4
                sl = t % 3
                xs = xin[:, sl * 1024:(sl + 1) * 1024]
                lt = ln1_tm[:, (t % 2) * 1024:(t % 2 + 1) * 1024]
                p.step("sync", lambda e, t=t, xs=xs, R0=R0: e.dma_start(
                    out=xs, in_=X[R0 + 128 * t: R0 + 128 * (t + 1), :]),
                    wait=rw(f"xin{sl}"), inc=[(lxs[sl], 16)], tag=f"s{s}ldx{t}")
                SS = STS[t % 2]
                scr = Ebuf.bitcast(F32)[:, 0:1024]
                p.step("scalar", lambda e, xs=xs, scr=scr, SS=SS: e.activation(
                    scr, xs, AF.Copy, accum_out=SS[:, 0:1]),
                    wait=[lxs[sl], (prnD, max(0, prnD.count - 5)),
                          (prnA, max(0, prnA.count - 2))]
                    + (rw("Ebuf") if t == 0 else []),
                    inc=[(prnA, 1)], tag=f"s{s}st1_{t}")
                p.step("scalar", lambda e, xs=xs, scr=scr, SS=SS: e.activation(
                    scr, xs, AF.Square, accum_out=SS[:, 1:2]),
                    wait=[prnA], inc=[(prnA, 1)], tag=f"s{s}st2_{t}")
                p.step("vector", lambda e, SS=SS: e.tensor_scalar(
                    SS[:, 3:4], SS[:, 0:1], 1.0 / 1024, None, ALU.mult),
                    wait=[prnA], inc=[(prnD, 1)], tag=f"s{s}st3_{t}")
                p.step("vector", lambda e, SS=SS: e.scalar_tensor_tensor(
                    SS[:, 2:3], SS[:, 3:4], 1.0, SS[:, 3:4], ALU.mult, ALU.mult),
                    wait=[prnD], inc=[(prnD, 1)], tag=f"s{s}st4_{t}")
                p.step("vector", lambda e, SS=SS: e.scalar_tensor_tensor(
                    SS[:, 4:5], SS[:, 1:2], 1.0 / 1024, SS[:, 2:3],
                    ALU.mult, ALU.subtract),
                    wait=[prnD], inc=[(prnD, 1)], tag=f"s{s}st5_{t}")
                p.step("scalar", lambda e, SS=SS: e.activation(
                    SS[:, 7:8], SS[:, 4:5], AF.Ln, bias=epsv),
                    wait=[prnD], inc=[(prnA, 1)], tag=f"s{s}st6_{t}")
                p.step("scalar", lambda e, SS=SS: e.activation(
                    SS[:, 5:6], SS[:, 7:8], AF.Exp, scale=-0.5),
                    wait=[prnA], inc=[(prnA, 1)], tag=f"s{s}st7_{t}")
                p.step("vector", lambda e, SS=SS: e.scalar_tensor_tensor(
                    SS[:, 6:7], SS[:, 3:4], -1.0, SS[:, 5:6], ALU.mult, ALU.mult),
                    wait=[prnA], inc=[(prnD, 1)], tag=f"s{s}st8_{t}")
                p.step("vector", lambda e, xs=xs, lt=lt, SS=SS: e.scalar_tensor_tensor(
                    lt, xs, SS[:, 5:6], SS[:, 6:7].broadcast_to([128, 1024]),
                    ALU.mult, ALU.add),
                    wait=[prnD] + rw(f"ln3st{t % 2}"), inc=[(prnD, 1)],
                    tag=f"s{s}nrm{t}")
                rel[f"xin{sl}"] = (prnD, prnD.count)
                for c in range(8):
                    bcol = (6 + c % 2) * 1024 + (c // 2) * 128
                    slot = ps_bf[:, bcol:bcol + 128]
                    w = [(pev, max(0, pev.count - 1))]
                    if c == 0:
                        w.append((prnD, prnD.count))
                        if t == 0:
                            w += rw("bank6") + rw("bank7")
                            if s == 0:
                                w.append(prep_ev)
                    p.step("tensor", lambda e, lt=lt, c=c, slot=slot: e.transpose(
                        slot, lt[:, c * 128:(c + 1) * 128], ident_bf),
                        wait=w, inc=[(ptr, 1)], tag=f"s{s}tr{t}_{c}")
                    p.step("vector", lambda e, c=c, t4=t4, slot=slot: e.tensor_copy(
                        ln1_bl[:, c * 512 + t4 * 128: c * 512 + (t4 + 1) * 128], slot),
                        wait=[ptr] + (rw("ln1bl") if c == 0 and t4 == 0 else []),
                        inc=[(pev, 1)], tag=f"s{s}ev{t}_{c}")
                marks[f"A{s}t{t}"] = len(p.steps)
            marks[f"A{s}g{g}p"] = len(p.steps)
            ldidx = 0
            for wnm, dest in (("wk", "K"), ("wq", "Q"), ("wv", "V")):
                for half in range(2):
                    wsl = (g * 6 + ldidx) % 2
                    ldidx += 1
                    lws = lw[wsl]
                    p.step("gpsimd", lambda e, wnm=wnm, half=half, wsl=wsl: e.dma_start(
                        out=Wslot[wsl],
                        in_=wbf[wnm][:, 512 * half:512 * (half + 1)]
                        .rearrange("(c p) o -> p c o", p=128)),
                        wait=rw(f"Wx{wsl}"), inc=[(lws, 16)], tag=f"s{s}lw{g}{wnm}{half}")
                    if dest in ("K", "Q"):
                        out_fm = K_fm if dest == "K" else Q_fm
                        bT = bkT if dest == "K" else bqT
                        for do4 in range(4):
                            DO = 4 * half + do4
                            bk_i = 4 + (kqi % 2)
                            for di in range(8):
                                w = []
                                if di == 0:
                                    w = [(lws, lws.count), (pev, pev.count),
                                         (kqe, max(0, kqe.count - 1))]
                                    if kqi == 0:
                                        w += rw("banks45")
                                        if s == 0:
                                            w.append(prep_ev)
                                p.step("tensor", lambda e, wsl=wsl, di=di, do4=do4, bk_i=bk_i: e.matmul(
                                    bank(bk_i),
                                    Wslot[wsl][:, di * 512 + do4 * 128: di * 512 + (do4 + 1) * 128],
                                    ln1_bl[:, di * 512:(di + 1) * 512],
                                    start=(di == 0), stop=(di == 7)),
                                    wait=w, inc=([(kqm, 1)] if di == 7 else []),
                                    tag=f"s{s}b{dest}{g}{half}{do4}_{di}")
                            p.step("vector", lambda e, out_fm=out_fm, DO=DO, bT=bT, bk_i=bk_i, g=g: e.tensor_scalar(
                                out_fm[:, DO * 2048 + g * 512: DO * 2048 + (g + 1) * 512],
                                bank(bk_i), bT[:, DO:DO + 1], None, ALU.add),
                                wait=[kqm] + (rw("arena", "arenab") if kqi == 0 else []),
                                inc=[(kqe, 1)], tag=f"s{s}e{dest}{g}{half}{do4}")
                            kqi += 1
                    else:
                        for t4 in range(4):
                            kt = 4 * g + t4
                            bk_i = vi % 4
                            for di in range(8):
                                w = []
                                if di == 0:
                                    w = [(lws, lws.count), (pev, pev.count),
                                         (vev, max(0, vev.count - 3))]
                                    if vi == 0:
                                        w += rw("banks03")
                                p.step("tensor", lambda e, wsl=wsl, di=di, t4=t4, bk_i=bk_i: e.matmul(
                                    bank(bk_i),
                                    ln1_bl[:, di * 512 + t4 * 128: di * 512 + (t4 + 1) * 128],
                                    Wslot[wsl][:, di * 512:(di + 1) * 512],
                                    start=(di == 0), stop=False),
                                    wait=w, tag=f"s{s}bV{g}{half}{t4}_{di}")
                            p.step("tensor", lambda e, half=half, bk_i=bk_i: e.matmul(
                                bank(bk_i), ones_bf[0:1, :],
                                bv_bf[0:1, 512 * half:512 * (half + 1)],
                                start=False, stop=True),
                                inc=[(kqm, 1)], tag=f"s{s}bVb{g}{half}{t4}")
                            for hh in range(8):
                                hgl = 8 * half + hh
                                p.step("vector", lambda e, kt=kt, hgl=hgl, hh=hh, bk_i=bk_i: e.tensor_copy(
                                    Vaug[:, kt * 1040 + hgl * 65: kt * 1040 + hgl * 65 + 64],
                                    ps[:, 512 * bk_i + hh * 64: 512 * bk_i + (hh + 1) * 64]),
                                    wait=([kqm] if hh == 0 else []),
                                    inc=([(vev, 1)] if hh == 7 else []),
                                    tag=f"s{s}eV{g}{half}{t4}_{hh}")
                            vi += 1
                    rel[f"Wx{wsl}"] = (kqm, kqm.count)
            rel["ln1bl"] = (kqm, kqm.count)
            marks[f"A{s}g{g}"] = len(p.steps)
        p.step("gpsimd", lambda e: e.dma_start(
            out=Wo_s, in_=wbf["wo"].rearrange("(c p) o -> p c o", p=128)),
            wait=rw("Wxo"), inc=[(lwo, 16)], tag=f"s{s}lwo")

        marks[f"A{s}"] = len(p.steps)
        builds_done = [(kqe, kqe.count), (vev, vev.count), (kqm, kqm.count),
                       (vns, vns.count)]

        def plan_pv_pair(u, jp, sxp_val):
            h = u % 16
            pr = jp % 2
            for j in (2 * jp, 2 * jp + 1):
                wpv = [(sxp, sxp_val)]
                if j == 0:
                    wpv.append((epi, epi.count))
                p.step("tensor", lambda e, j=j, pr=pr, u=u, h=h: e.matmul(
                    ps[0:65, 2048 + 512 * (u % 2): 2048 + 512 * (u % 2) + 512],
                    Vaug[:, j * 1040 + h * 65: j * 1040 + h * 65 + 65],
                    Ebuf[:, 1024 * pr + 512 * (j % 2): 1024 * pr + 512 * (j % 2) + 512],
                    start=(j == 0), stop=(j == 15)),
                    wait=wpv, inc=([(pvm, 1)] if j % 2 == 1 else []),
                    tag=f"s{s}pv{u}_{j}")

        def plan_unit(u):
            g, h = u // 16, u % 16
            hc, hr = h // 2, 64 * (h % 2)
            hrc = 64 - hr
            z = u % 2
            wz = [(smm, max(0, smm.count - 8))]
            if u == 0:
                wz += builds_done
            p.step("vector", lambda e, z=z, hrc=hrc: e.memset(
                Q_z[hrc:hrc + 64, z * 512:(z + 1) * 512], 0.0),
                wait=wz, inc=[(qz, 1)], tag=f"s{s}qz0_{u}")
            p.step("vector", lambda e, z=z, hr=hr, hc=hc, g=g: e.tensor_copy(
                Q_z[hr:hr + 64, z * 512:(z + 1) * 512],
                Q_fm[hr:hr + 64, hc * 2048 + g * 512: hc * 2048 + (g + 1) * 512]),
                inc=[(qz, 1)], tag=f"s{s}qz1_{u}")
            for jp in range(8):
                P = u * 8 + jp
                pr = P % 2
                for j in (2 * jp, 2 * jp + 1):
                    w = []
                    if j % 2 == 0:
                        w.append((sxp, max(0, sxp.count - 1)))
                    if jp == 0 and j % 2 == 0:
                        w.append((qz, qz.count))
                    p.step("tensor", lambda e, j=j, pr=pr, hc=hc, z=z: e.matmul(
                        ps[:, 1024 * pr + 512 * (j % 2): 1024 * pr + 512 * (j % 2) + 512],
                        K_fm[:, hc * 2048 + j * 128: hc * 2048 + (j + 1) * 128],
                        Q_z[:, z * 512:(z + 1) * 512],
                        start=True, stop=True),
                        wait=w, inc=([(smm, 1)] if j % 2 == 1 else []),
                        tag=f"s{s}sT{u}_{j}")
                p.step("scalar", lambda e, pr=pr: e.activation(
                    Ebuf[:, 1024 * pr:1024 * (pr + 1)],
                    ps[:, 1024 * pr:1024 * (pr + 1)], AF.Exp, scale=0.125),
                    wait=[(smm, smm.count), (pvm, pvm.count)],
                    inc=[(sxp, 1)], tag=f"s{s}xp{u}_{P}")
                if PVLAG:
                    if jp >= 1:
                        plan_pv_pair(u, jp - 1, sxp.count - 1)
                else:
                    plan_pv_pair(u, jp, sxp.count)
            if PVLAG:
                plan_pv_pair(u, 7, sxp.count)

        def plan_epi1(u):
            p.step("vector", lambda e, u=u: e.tensor_copy(
                srow, ps[64:65, 2048 + 512 * (u % 2): 2048 + 512 * (u % 2) + 512]),
                wait=[(pvm, pvm.count), (epi, epi.count)],
                inc=[(epi, 1)], tag=f"s{s}e1_{u}")

        def plan_epi2(u):
            g, h = u // 16, u % 16
            hc, hr = h // 2, 64 * (h % 2)
            gb = (g % 2) * 4096
            p.step("vector", lambda e: e.reciprocal(rs, srow),
                   wait=[epi], inc=[(epi, 1)], tag=f"s{s}e2_{u}")
            if E4BF16:
                p.step("vector", lambda e: e.tensor_copy(rs_bf, rs),
                       wait=[epi], inc=[(epi, 1)], tag=f"s{s}e3b_{u}")
                p.step("tensor", lambda e: e.matmul(
                    ps[0:64, 3072:3584], ones_bf[0:1, 0:64], rs_bf[0:1, :],
                    start=True, stop=True),
                    wait=[epi], inc=[(epi, 1)], tag=f"s{s}e4_{u}")
            else:
                p.step("tensor", lambda e: e.matmul(
                    ps[0:64, 3072:3584], ones64f[0:1, :], rs[0:1, :],
                    start=True, stop=True),
                    wait=[epi], inc=[(epi, 1)], tag=f"s{s}e4_{u}")
            p.step("vector", lambda e: e.tensor_copy(rsb, ps[0:64, 3072:3584]),
                   wait=[epi], inc=[(epi, 1)], tag=f"s{s}e5_{u}")
            p.step("vector", lambda e, u=u, hr=hr, hc=hc, gb=gb: e.scalar_tensor_tensor(
                attn_fm[hr:hr + 64, gb + hc * 512: gb + (hc + 1) * 512],
                ps[0:64, 2048 + 512 * (u % 2): 2048 + 512 * (u % 2) + 512],
                1.0, rsb, ALU.mult, ALU.mult),
                wait=[epi, (orw, 8 * max(0, u // 16 - 1))],
                inc=[(epi, 1)], tag=f"s{s}e6_{u}")

        def plan_otile(g, t4, nlist=(0, 1)):
            gb = (g % 2) * 4096
            t = 4 * g + t4
            sl = t % 2
            for n in nlist:
                for di in range(8):
                    w = []
                    if di == 0:
                        w = [(epi, epi_group_done[g]), (lwo, lwo.count),
                             (oev, oev.count)]
                        if g == 0 and t4 == 0 and n == 0:
                            w += rw("bank7")
                    p.step("tensor", lambda e, di=di, t4=t4, n=n, gb=gb: e.matmul(
                        bank(7),
                        attn_fm[:, gb + di * 512 + t4 * 128: gb + di * 512 + (t4 + 1) * 128],
                        Wo_s[:, di * 1024 + n * 512: di * 1024 + (n + 1) * 512],
                        start=(di == 0), stop=False),
                        wait=w, tag=f"s{s}op{g}{t4}{n}_{di}")
                p.step("tensor", lambda e, n=n: e.matmul(
                    bank(7), ones_bf[0:1, :], bo_bf[0:1, n * 512:(n + 1) * 512],
                    start=False, stop=True),
                    inc=[(orw, 1)], tag=f"s{s}opb{g}{t4}{n}")
                p.step("scalar", lambda e, sl=sl, n=n: e.activation(
                    o_tm[:, sl * 1024 + n * 512: sl * 1024 + (n + 1) * 512],
                    bank(7), AF.Copy),
                    wait=[(orw, orw.count)] + (rw(f"otm{sl}") if n == 0 else []),
                    inc=[(oev, 1)], tag=f"s{s}oe{g}{t4}{n}")

        def plan_ctile(g, t4):
            t = 4 * g + t4
            sl = t % 2
            xsl = t % 3
            ot = o_tm[:, sl * 1024:(sl + 1) * 1024]
            xs = xin[:, xsl * 1024:(xsl + 1) * 1024]
            xv = x1st[:, sl * 1024:(sl + 1) * 1024]
            lv = ln3st[:, sl * 1024:(sl + 1) * 1024]
            p.step("sync", lambda e, t=t, xs=xs, R0=R0: e.dma_start(
                out=xs, in_=X[R0 + 128 * t: R0 + 128 * (t + 1), :]),
                wait=rw(f"xin{xsl}"), inc=[(lxs[xsl], 16)], tag=f"s{s}rx{t}")
            ln_stats(ce, ot, waits=[(oev, oev.count)] + rw("ln_scalars"))
            ln_scalar_chain(ce)
            p.step("vector", lambda e, ot=ot: e.scalar_tensor_tensor(
                o_n, ot, rst, nmr.broadcast_to([128, 1024]), ALU.mult, ALU.add),
                wait=[ce] + rw("o_n"), inc=[(ce, 1)], tag=f"s{s}on{t}")
            rel[f"otm{sl}"] = (ce, ce.count)
            rel["ln_scalars"] = (ce, ce.count)
            p.step("vector", lambda e, xv=xv, xs=xs: e.tensor_add(xv, o_n, xs),
                   wait=[ce, (lxs[xsl], lxs[xsl].count)] + rw(f"x1st{sl}"),
                   inc=[(ce, 1)], tag=f"s{s}x1{t}")
            rel["o_n"] = (ce, ce.count)
            rel[f"xin{xsl}"] = (ce, ce.count)
            ln_stats(ce, xv)
            ln_scalar_chain(ce)
            p.step("vector", lambda e, xv=xv, lv=lv: e.scalar_tensor_tensor(
                lv, xv, rst, nmr.broadcast_to([128, 1024]), ALU.mult, ALU.add),
                wait=[ce] + rw(f"ln3st{sl}"), inc=[(ce, 1)], tag=f"s{s}l3{t}")
            rel["ln_scalars"] = (ce, ce.count)
            p.step("sync", lambda e, t=t, xv=xv, R0=R0: e.dma_start(
                out=x1_s[R0 + 128 * t: R0 + 128 * (t + 1), :], in_=xv),
                wait=[(ce, ce.count)], inc=[(cs_x1[sl], 16)], tag=f"s{s}sx1{t}")
            rel[f"x1st{sl}"] = (cs_x1[sl], cs_x1[sl].count)
            p.step("sync", lambda e, t=t, lv=lv, R0=R0: e.dma_start(
                out=ln3_s[R0 + 128 * t: R0 + 128 * (t + 1), :], in_=lv),
                inc=[(cs_l3[sl], 16)], tag=f"s{s}sl3{t}")
            rel[f"ln3st{sl}"] = (cs_l3[sl], cs_l3[sl].count)

        ln3_ready = {}
        epi_group_done = {}
        for u in range(65):
            if u < 64:
                plan_unit(u)
            if u >= 1:
                plan_epi2(u - 1)
                if (u - 1) % 16 == 15:
                    epi_group_done[(u - 1) // 16] = epi.count
            if u < 64:
                plan_epi1(u)
            gp = u // 16 - 1
            idx = u % 16
            if SPREAD_OPROJ:
                if gp >= 0 and 1 <= idx <= 8:
                    t4, n = (idx - 1) // 2, (idx - 1) % 2
                    plan_otile(gp, t4, nlist=(n,))
                    if n == 1:
                        plan_ctile(gp, t4)
                        if t4 == 3:
                            ln3_ready[gp] = [(cs_l3[0], cs_l3[0].count),
                                             (cs_l3[1], cs_l3[1].count)]
                if u == 64:
                    for t4 in range(4):
                        plan_otile(3, t4)
                        plan_ctile(3, t4)
                    ln3_ready[3] = [(cs_l3[0], cs_l3[0].count),
                                    (cs_l3[1], cs_l3[1].count)]
            else:
                if u >= 1 and (u - 1) % 16 == 15:
                    g = (u - 1) // 16
                    for t4 in range(4):
                        plan_otile(g, t4)
                        plan_ctile(g, t4)
                    ln3_ready[g] = [(cs_l3[0], cs_l3[0].count),
                                    (cs_l3[1], cs_l3[1].count)]
        marks[f"B{s}"] = len(p.steps)
        rel["Ebuf"] = (pvm, pvm.count)
        attn_done = [(sxp, sxp.count), (epi, epi.count), (oev, oev.count),
                     (orw, orw.count), (smm, smm.count)]

        rel["fw0"] = (smm, smm.count)
        rel["fw1"] = (smm, smm.count)
        rel["ln3bl"] = (smm, smm.count)
        rel["accH"] = (epi, epi.count)

        hsg = 0
        ggi = 0
        for Hh in range(2):
            for u2 in range(2):
                for di in range(8):
                    w = list(ln3_ready[2 * Hh + u2])
                    if u2 == 0 and di == 0:
                        w += rw("ln3bl")
                        w.append((fmm, fmm.count))
                    p.step("sync", lambda e, Hh=Hh, u2=u2, di=di, R0=R0: e.dma_start_transpose(
                        ln3_bl[:, di * 1024 + u2 * 512: di * 1024 + (u2 + 1) * 512],
                        ln3_s[R0 + 1024 * Hh + 512 * u2: R0 + 1024 * Hh + 512 * (u2 + 1),
                              di * 128:(di + 1) * 128]),
                        wait=w, inc=[(fln, 16)], tag=f"s{s}fl{Hh}{u2}_{di}")
            rel["ln1bl"] = (fln, fln.count)
            fln_done = fln.count
            fge_done = {}
            down_pending = []

            def plan_down(job):
                hs, hsl, fsl = job
                for tt in range(8):
                    dbk = 4 + 2 * (tt % 2)
                    at = acc[:, 1024 * tt:1024 * (tt + 1)]
                    for c in range(2):
                        for n in range(2):
                            w = []
                            if c == 0 and n == 0:
                                w = [(fge, fge_done[hs]),
                                     (fda, max(0, fda.count - 1))]
                                if hs == 0 and tt < 2:
                                    w.append((fln, fln_done))
                                    if Hh == 0:
                                        w += attn_done
                            p.step("tensor", lambda e, c=c, n=n, tt=tt, hsl=hsl, fsl=fsl, dbk=dbk: e.matmul(
                                bank(dbk + n),
                                h_sfm[hsl][:, c * 1024 + tt * 128: c * 1024 + (tt + 1) * 128],
                                wdT_s[fsl][:, c * 1024 + n * 512: c * 1024 + (n + 1) * 512],
                                start=(c == 0), stop=(c == 1)),
                                wait=w,
                                inc=([(fdm, 1)] if c == 1 and n == 1 else []),
                                tag=f"s{s}d{Hh}{hs}{tt}_{c}{n}")
                    if hs == 0:
                        p.step("vector", lambda e, at=at, dbk=dbk: e.tensor_scalar(
                            at, bank2(dbk), QSC, None, ALU.mult),
                            wait=[(fdm, fdm.count)] + (rw("accH") if tt == 0 and Hh == 0 else []),
                            inc=[(fda, 1)], tag=f"s{s}da{Hh}{hs}{tt}")
                    else:
                        p.step("vector", lambda e, at=at, dbk=dbk: e.scalar_tensor_tensor(
                            at, bank2(dbk), QSC, at, ALU.mult, ALU.add),
                            wait=[(fdm, fdm.count)],
                            inc=[(fda, 1)], tag=f"s{s}da{Hh}{hs}{tt}")
                rel[f"fw{fsl}"] = (fdm, fdm.count)

            fln_done = fln.count
            for hs in range(12):
                fsl = hsg % 2
                hof = hs * 256
                fws = fw[fsl]
                for wsrc, wdst in ((wgT_bf, wgT_s), (wuT_bf, wuT_s)):
                    p.step("gpsimd", lambda e, wsrc=wsrc, wdst=wdst, fsl=fsl, hof=hof: e.dma_start(
                        out=wdst[fsl],
                        in_=wsrc[:, hof:hof + 256].rearrange("(c p) h -> p c h", p=128)),
                        wait=rw(f"fw{fsl}"), inc=[(fws, 16)], tag=f"s{s}fw{Hh}{hs}")
                p.step("gpsimd", lambda e, fsl=fsl, hof=hof: e.dma_start(
                    out=wdT_s[fsl],
                    in_=wdT_bf[hof:hof + 256, :].rearrange("(c p) d -> p c d", p=128)),
                    inc=[(fws, 16)], tag=f"s{s}fwd{Hh}{hs}")
                hsl = hsg % 2
                for u2 in range(2):
                    for c in range(2):
                        gbk = ggi % 2
                        ubk = 2 + ggi % 2
                        gsl = ggi % 2
                        for di in range(8):
                            w = []
                            if di == 0:
                                w = [(fws, fws.count), (fln, fln.count),
                                     (fge, max(0, fge.count - 2))]
                                if Hh == 0 and hs == 0 and u2 == 0 and c == 0:
                                    w += attn_done
                            p.step("tensor", lambda e, di=di, c=c, u2=u2, fsl=fsl, gbk=gbk: e.matmul(
                                bank(gbk),
                                wgT_s[fsl][:, di * 256 + c * 128: di * 256 + (c + 1) * 128],
                                ln3_bl[:, di * 1024 + u2 * 512: di * 1024 + (u2 + 1) * 512],
                                start=(di == 0), stop=(di == 7)),
                                wait=w, tag=f"s{s}g{Hh}{hs}{u2}{c}_{di}")
                            p.step("tensor", lambda e, di=di, c=c, u2=u2, fsl=fsl, ubk=ubk: e.matmul(
                                bank(ubk),
                                wuT_s[fsl][:, di * 256 + c * 128: di * 256 + (c + 1) * 128],
                                ln3_bl[:, di * 1024 + u2 * 512: di * 1024 + (u2 + 1) * 512],
                                start=(di == 0), stop=(di == 7)),
                                inc=([(fmm, 1)] if di == 7 else []),
                                tag=f"s{s}uu{Hh}{hs}{u2}{c}_{di}")
                        p.step("scalar", lambda e, gbk=gbk, gsl=gsl: e.activation(
                            gbf[:, gsl * 512:(gsl + 1) * 512], bank(gbk), AF.Silu,
                            scale=QSC),
                            wait=[(fmm, fmm.count), (fge, fge.count)],
                            inc=[(fge, 1)], tag=f"s{s}sl{Hh}{hs}{u2}{c}")
                        p.step("vector", lambda e, ubk=ubk, gsl=gsl, hsl=hsl, c=c, u2=u2: e.scalar_tensor_tensor(
                            h_sfm[hsl][:, c * 1024 + u2 * 512: c * 1024 + (u2 + 1) * 512],
                            bank(ubk), QSC, gbf[:, gsl * 512:(gsl + 1) * 512],
                            ALU.mult, ALU.mult),
                            wait=[(fge, fge.count), (fdm, fdm.count)],
                            inc=[(fge, 1)], tag=f"s{s}h{Hh}{hs}{u2}{c}")
                        ggi += 1
                fge_done[hs] = fge.count
                down_pending.append((hs, hsl, fsl))
                if hs >= 1:
                    plan_down(down_pending.pop(0))
                if hs == 11:
                    plan_down(down_pending.pop(0))
                hsg += 1
            for tt in range(8):
                t = 8 * Hh + tt
                sl = t % 2
                at = acc[:, 1024 * tt:1024 * (tt + 1)]
                ov = o4[:, sl * 1024:(sl + 1) * 1024]
                xs = x1st[:, sl * 1024:(sl + 1) * 1024]
                p.step("sync", lambda e, t=t, xs=xs, R0=R0: e.dma_start(
                    out=xs, in_=x1_s[R0 + 128 * t: R0 + 128 * (t + 1), :]),
                    wait=rw(f"x1st{sl}") + [(cs_x1[sl], cs_x1[sl].count)],
                    inc=[(lxi[sl], 16)], tag=f"s{s}fx1{t}")
                ln_stats(fep, at, waits=[(fda, fda.count)] + rw("ln_scalars"))
                ln_scalar_chain(fep)
                p.step("vector", lambda e, at=at, ov=ov: e.scalar_tensor_tensor(
                    ov, at, rst, nmr.broadcast_to([128, 1024]), ALU.mult, ALU.add),
                    wait=[fep] + rw(f"o4{sl}"), inc=[(fep, 1)], tag=f"s{s}f4{t}")
                rel["ln_scalars"] = (fep, fep.count)
                p.step("vector", lambda e, ov=ov, xs=xs: e.tensor_add(ov, ov, xs),
                       wait=[fep, (lxi[sl], lxi[sl].count)], inc=[(fep, 1)],
                       tag=f"s{s}f5{t}")
                rel[f"x1st{sl}"] = (fep, fep.count)
                p.step("sync", lambda e, t=t, ov=ov, R0=R0: e.dma_start(
                    out=OUT[R0 + 128 * t: R0 + 128 * (t + 1), :], in_=ov),
                    wait=[(fep, fep.count)], inc=[(st2[sl], 16)], tag=f"s{s}out{t}")
                rel[f"o4{sl}"] = (st2[sl], st2[sl].count)
            rel["accH"] = (fep, fep.count)

        marks[f"D{s}"] = len(p.steps)
        rel["arena"] = (st2[1], st2[1].count)
        rel["arenab"] = (st2[0], st2[0].count)
        rel["bank6"] = (fda, fda.count)
        rel["bank7"] = (fda, fda.count)
        rel["banks03"] = (fge, fge.count)
        rel["banks45"] = (fda, fda.count)
        rel["Wxo"] = (orw, orw.count)

    if truncate_at is not None:
        p.steps = p.steps[:truncate_at]
    elif truncate is not None:
        p.steps = p.steps[:marks[truncate]]
    else:
        p.step("sync", lambda e: e.nop(),
               wait=[(stx, stx.count) for stx in st_sems], tag="final")

    build_program.marks = dict(marks)
    build_program.plan = p
    p.emit(nc, es)
    es.close()
    return nc


_NC_CACHE = None


def kernel(L0_attn_x, L0_attn_Wq, L0_attn_bq, L0_attn_Wk, L0_attn_bk,
           L0_attn_Wv, L0_attn_bv, L0_attn_Wo, L0_attn_bo,
           L0_ffn_w_gate, L0_ffn_w_up, L0_ffn_w_down):
    global _NC_CACHE
    if _NC_CACHE is None:
        _NC_CACHE = build_program()
    nc = _NC_CACHE

    x = np.ascontiguousarray(np.asarray(L0_attn_x, dtype=np.float32))
    base = dict(
        wq=np.ascontiguousarray(np.asarray(L0_attn_Wq, dtype=np.float32)),
        wk=np.ascontiguousarray(np.asarray(L0_attn_Wk, dtype=np.float32)),
        wv=np.ascontiguousarray(np.asarray(L0_attn_Wv, dtype=np.float32)),
        wo=np.ascontiguousarray(np.asarray(L0_attn_Wo, dtype=np.float32)),
        bq=np.ascontiguousarray(np.asarray(L0_attn_bq, dtype=np.float32)),
        bk=np.ascontiguousarray(np.asarray(L0_attn_bk, dtype=np.float32)),
        bv=np.ascontiguousarray(np.asarray(L0_attn_bv, dtype=np.float32)),
        bo=np.ascontiguousarray(np.asarray(L0_attn_bo, dtype=np.float32)),
        wgi=np.ascontiguousarray(np.asarray(L0_ffn_w_gate, dtype=np.int32)),
        wui=np.ascontiguousarray(np.asarray(L0_ffn_w_up, dtype=np.int32)),
        wdi=np.ascontiguousarray(np.asarray(L0_ffn_w_down, dtype=np.int32)),
    )
    in_maps = []
    for i in range(8):
        m = dict(base)
        m["x"] = np.ascontiguousarray(
            x[NSEQ * i:NSEQ * (i + 1)].reshape(NSEQ * S, D))
        in_maps.append(m)

    res = run_bass_kernel_spmd(nc, in_maps, core_ids=list(range(8)))
    outs = [np.asarray(r["out"]).reshape(NSEQ, S, D) for r in res.results]
    return np.concatenate(outs, axis=0).astype(np.float32)
